# version 19
# baseline (speedup 1.0000x reference)
import numpy as np
from contextlib import ExitStack

import concourse.bass as bass
import concourse.mybir as mybir
from concourse.bass_utils import run_bass_kernel_spmd

F32 = mybir.dt.float32
BF16 = mybir.dt.bfloat16
AF = mybir.ActivationFunctionType
ALU = mybir.AluOpType
AX = mybir.AxisListType

C = 2048
N = 2048
CT = 16
NT = 16
NG = 4
JT = 16
NE = 8
NF = 16
NCQ = 4
NTH = 8
EPS_ATTN = 1e-15
EPS_LN = 1e-5


class Sem:
    def __init__(self, handle):
        self.h = handle
        self.n = 0


class Prog:
    ENGINES = ("tensor", "vector", "scalar", "gpsimd", "sync")

    def __init__(self):
        self.ops = {e: [] for e in self.ENGINES}
        self.bank_guard = {}
        self.slot_guard = {}

    def op(self, eng, fn, waits=(), inc=None, amt=1):
        best = {}
        for (s, v) in waits:
            if s is not None and v > 0:
                if best.get(id(s), (None, -1))[1] < v:
                    best[id(s)] = (s, v)
        waits = list(best.values())
        if inc is not None:
            inc.n += amt
        self.ops[eng].append((waits, fn, inc, amt))

    def bank_wait(self, bank):
        g = self.bank_guard.get(bank)
        return [g] if g else []

    def bank_free(self, bank, sem):
        self.bank_guard[bank] = (sem, sem.n)

    def slot_wait(self, key):
        g = self.slot_guard.get(key)
        return [g] if g else []

    def slot_free(self, key, sem):
        self.slot_guard[key] = (sem, sem.n)

    def run(self, eng_name, eng):
        for waits, fn, inc, amt in self.ops[eng_name]:
            for s, v in waits:
                eng.wait_ge(s.h, v)
            ins = fn(eng)
            if inc is not None:
                if ins is None:
                    ins = eng.nop()
                ins.then_inc(inc.h, amt)


def build_program(debug=False):
    nc = bass.Bass("TRN2", target_bir_lowering=False)

    xT = nc.dram_tensor("xT", [C, N], F32, kind="ExternalInput").ap()
    wq_d = nc.dram_tensor("wq", [JT, 128, 2048], F32, kind="ExternalInput").ap()
    wk_d = nc.dram_tensor("wk", [NG, 128, 8192], F32, kind="ExternalInput").ap()
    wv_d = nc.dram_tensor("wv", [NG, 128, 8192], F32, kind="ExternalInput").ap()
    wp_d = nc.dram_tensor("wp", [CT, 128, 2048], F32, kind="ExternalInput").ap()
    bp_d = nc.dram_tensor("bp", [128, 16], F32, kind="ExternalInput").ap()
    wr_d = nc.dram_tensor("wr", [128, 128], F32, kind="ExternalInput").ap()
    wg_d = nc.dram_tensor("wg", [NE, NF, 128, 2048], F32, kind="ExternalInput").ap()
    wu_d = nc.dram_tensor("wu", [NE, NF, 128, 2048], F32, kind="ExternalInput").ap()
    wd_d = nc.dram_tensor("wd", [NE, NCQ, 128, 8192], F32, kind="ExternalInput").ap()
    id_d = nc.dram_tensor("id_in", [128, 128], F32, kind="ExternalInput").ap()
    er_d = nc.dram_tensor("er_in", [32, 512], F32, kind="ExternalInput").ap()
    fb_d = nc.dram_tensor("fb_in", [4, 128], F32, kind="ExternalInput").ap()
    onr_d = nc.dram_tensor("onr_in", [1, 128], F32, kind="ExternalInput").ap()
    onc_d = nc.dram_tensor("onc_in", [128, 1], F32, kind="ExternalInput").ap()
    out_d = nc.dram_tensor("out", [N, C], F32, kind="ExternalOutput").ap()
    dbg = {}
    if debug:
        dbg["x1"] = nc.dram_tensor("dbg_x1", [128, CT * 2048], F32, kind="ExternalOutput").ap()
        dbg["attn"] = nc.dram_tensor("dbg_attn", [128, CT * 2048], F32, kind="ExternalOutput").ap()
        dbg["w"] = nc.dram_tensor("dbg_w", [128, 128], F32, kind="ExternalOutput").ap()

    with ExitStack() as stack:
        e = stack.enter_context

        STREAM = e(nc.sbuf_tensor("STREAM", [128, CT * 2048], BF16))
        BUF2 = e(nc.sbuf_tensor("BUF2", [128, CT * 2048], BF16))
        SCR = e(nc.sbuf_tensor("SCR", [128, 24576], BF16))
        OUT_SB = e(nc.sbuf_tensor("OUT_SB", [128, 2048], F32))
        GS = e(nc.sbuf_tensor("GS", [128, 1024], BF16))
        QSB = e(nc.sbuf_tensor("QSB", [128, 1024], BF16))
        TAPP = QSB[:, 512:1024]
        SFLAT = e(nc.sbuf_tensor("SFLAT", [32, 2112], BF16))
        SNB = e(nc.sbuf_tensor("SNB", [128, 264], BF16))
        DENS = e(nc.sbuf_tensor("DENS", [4, 512], BF16))
        DENB = e(nc.sbuf_tensor("DENB", [128, 512], BF16))
        WSB = e(nc.sbuf_tensor("WSB", [128, 128], F32))
        SCSB = e(nc.sbuf_tensor("SCSB", [128, 128], F32))
        MROW = e(nc.sbuf_tensor("MROW", [1, 2048], BF16))
        RROW = e(nc.sbuf_tensor("RROW", [1, 2048], BF16))
        BSB = e(nc.sbuf_tensor("BSB", [128, 16], F32))
        WRS = e(nc.sbuf_tensor("WRS", [128, 128], BF16))
        IDN = e(nc.sbuf_tensor("IDN", [128, 128], BF16))
        ERS = e(nc.sbuf_tensor("ERS", [32, 512], BF16))
        FBS = e(nc.sbuf_tensor("FBS", [4, 128], BF16))
        ONR = e(nc.sbuf_tensor("ONR", [1, 128], BF16))
        ONC = e(nc.sbuf_tensor("ONC", [128, 1], BF16))
        SM1 = e(nc.sbuf_tensor("SM1", [128, 1], F32))
        SM2 = e(nc.sbuf_tensor("SM2", [128, 1], F32))
        SM3 = e(nc.sbuf_tensor("SM3", [128, 1], F32))
        SE8 = e(nc.sbuf_tensor("SE8", [128, 8], F32))
        ST8 = e(nc.sbuf_tensor("ST8", [128, 8], F32))
        SL8 = e(nc.sbuf_tensor("SL8", [128, 8], F32))
        CM = e(nc.sbuf_tensor("CM", [128, 1], F32))
        CV = e(nc.sbuf_tensor("CV", [128, 1], F32))
        CS = e(nc.sbuf_tensor("CS", [128, 1], F32))
        CB = e(nc.sbuf_tensor("CB", [128, 1], F32))

        PA = e(nc.psum_tensor("PA", [128, 1024], F32))
        PB = e(nc.psum_tensor("PB", [128, 1024], F32))
        PC = e(nc.psum_tensor("PC", [128, 1024], F32))
        PD = e(nc.psum_tensor("PD", [128, 2048], BF16))
        PDF = PD[:].bitcast(F32)

        sem_names = ["s_x", "s_wkvK", "s_wkvV", "s_wq0", "s_wq1", "s_wp0",
                     "s_wp1", "s_wg0", "s_wg1", "s_wu0", "s_wu1", "s_wd0",
                     "s_wd1", "s_ld", "s_p1", "s_kev", "s_vev", "s_smm", "s_sf",
                     "s_nbm", "s_nbe", "s_qmm", "s_qrl", "s_nmm", "s_dmm",
                     "s_dev", "s_bcm", "s_bce", "s_att", "s_pmm", "s_pev",
                     "s_sxm", "s_sqm", "s_row", "s_bc2", "s_ap", "s_sqc",
                     "s_sq", "s_rmm", "s_re", "s_tr", "s_tre", "s_gmm",
                     "s_sil", "s_umm", "s_h", "s_ymm", "s_yev", "s_csq",
                     "s_cr", "s_cact", "s_out", "s_dbg", "s_vch"]
        SD = {}
        for name in sem_names:
            SD[name] = Sem(e(nc.semaphore(name)))
        S = type("SS", (), SD)

        P = Prog()

        def dma(sem, dst_ap, src_ap, extra_waits=()):
            P.op("gpsimd", lambda g, d=dst_ap, s=src_ap: g.dma_start(out=d, in_=s),
                 waits=extra_waits, inc=sem, amt=16)
            return (sem, sem.n)

        def dma2(sem, dst_ap, src_ap, extra_waits=()):
            n = dst_ap.shape[-1]
            P.op("gpsimd", lambda g, d=dst_ap, s=src_ap: g.dma_start(
                    out=d[:, 0:n // 2], in_=s[:, 0:n // 2]),
                 waits=extra_waits, inc=sem, amt=16)
            P.op("gpsimd", lambda g, d=dst_ap, s=src_ap: g.dma_start(
                    out=d[:, n // 2:n], in_=s[:, n // 2:n]),
                 inc=sem, amt=16)
            return (sem, sem.n)

        WKV_K = SCR[:, 0:8192]
        WKV_V = SCR[:, 8192:16384]
        def WQ(slot): return SCR[:, 16384 + slot * 2048: 16384 + (slot + 1) * 2048]
        def WP(slot): return SCR[:, slot * 2048:(slot + 1) * 2048]
        def SQA(slot): return SCR[:, 8192 + (slot % 8) * 2048: 8192 + (slot % 8 + 1) * 2048]
        def WGB(slot): return SCR[:, slot * 2048:(slot + 1) * 2048]
        def WUB(slot): return SCR[:, 4096 + slot * 2048: 4096 + (slot + 1) * 2048]
        def WDB(slot): return SCR[:, 8192 + slot * 8192: 8192 + (slot + 1) * 8192]

        KTM = BUF2[:, 0:8192]
        VPAD = BUF2[:, 8192:16640]
        def ATTN(cc): return BUF2[:, cc * 2048:(cc + 1) * 2048]
        def YACC(half, ntl):
            return BUF2[:, half * 16384 + ntl * 2048: half * 16384 + (ntl + 1) * 2048]
        def HB(half, ft):
            base = (1 - half) * 16384
            return BUF2[:, base + ft * 1024: base + (ft + 1) * 1024]

        for cc in range(CT):
            dma(S.s_x, STREAM[:, cc * 2048:(cc + 1) * 2048],
                xT[cc * 128:(cc + 1) * 128, :])
        for d, s in [(IDN[:], id_d), (ERS[:], er_d), (FBS[:], fb_d),
                     (ONR[:], onr_d), (ONC[:], onc_d), (WRS[:], wr_d)]:
            dma(S.s_x, d, s)
        consts_done = (S.s_x, S.s_x.n)
        P.op("sync", lambda g: g.dma_start(out=BSB[:], in_=bp_d), inc=S.s_ld, amt=16)
        bp_done = S.s_ld.n
        P.op("vector", lambda v: v.memset(VPAD, 1.0), inc=S.s_vev, amt=1)
        vpad_ones = S.s_vev.n

        kv_group = 0
        wk_done = [None] * NG
        wv_done = [None] * NG
        wk_done[0] = dma2(S.s_wkvK, WKV_K, wk_d[0])
        wv_done[0] = dma2(S.s_wkvV, WKV_V, wv_d[0])

        for g in range(NG):
            for nt in range(NT):
                slot = kv_group % 2
                bank = ("PA", slot)
                ps = PA[:, slot * 512:(slot + 1) * 512]
                waits = [wk_done[g], consts_done] + P.bank_wait(bank)
                def mmk(t, ps=ps, nt=nt):
                    last = None
                    for cc in range(CT):
                        last = t.matmul(ps,
                                        STREAM[:, cc * 2048 + nt * 128: cc * 2048 + (nt + 1) * 128],
                                        WKV_K[:, cc * 512:(cc + 1) * 512],
                                        start=(cc == 0), stop=(cc == CT - 1))
                    return last
                P.op("tensor", mmk, waits=waits, inc=S.s_p1)
                need = S.s_p1.n
                P.op("scalar", lambda s, ps=ps, nt=nt: s.activation(
                        KTM[:, nt * 512:(nt + 1) * 512], ps, AF.Relu),
                     waits=[(S.s_p1, need), (S.s_vev, vpad_ones)], inc=S.s_kev)
                P.bank_free(bank, S.s_kev)
                kv_group += 1
            P.slot_free("wkvK", S.s_kev)
            if g + 1 < NG:
                wk_done[g + 1] = dma2(S.s_wkvK, WKV_K, wk_d[g + 1], extra_waits=P.slot_wait("wkvK"))
            for nt in range(NT):
                slot = kv_group % 2
                bank = ("PA", slot)
                ps = PA[:, slot * 512:(slot + 1) * 512]
                waits = [wv_done[g]] + P.bank_wait(bank)
                def mmv(t, ps=ps, nt=nt):
                    last = None
                    for cc in range(CT):
                        last = t.matmul(ps,
                                        STREAM[:, cc * 2048 + nt * 128: cc * 2048 + (nt + 1) * 128],
                                        WKV_V[:, cc * 512:(cc + 1) * 512],
                                        start=(cc == 0), stop=(cc == CT - 1))
                    return last
                P.op("tensor", mmv, waits=waits, inc=S.s_p1)
                need = S.s_p1.n
                def vev(v, ps=ps, nt=nt):
                    dst = VPAD[:, nt * 528:(nt + 1) * 528].rearrange(
                        "p (h e) -> p h e", e=33)[:, :, 0:32]
                    src = ps.rearrange("p (h e) -> p h e", e=32)
                    return v.tensor_copy(dst, src)
                P.op("vector", vev, waits=[(S.s_p1, need)], inc=S.s_vev)
                P.bank_free(bank, S.s_vev)
                kv_group += 1
            P.slot_free("wkvV", S.s_vev)
            if g + 1 < NG:
                wv_done[g + 1] = dma2(S.s_wkvV, WKV_V, wv_d[g + 1], extra_waits=P.slot_wait("wkvV"))
            kv_k_all, kv_v_all = S.s_kev.n, S.s_vev.n
            for hh in range(16):
                bsel = hh // 8
                bank = ("PB", bsel)
                col0 = bsel * 512 + (hh % 8) * 33
                ps = PB[0:32, col0: col0 + 33]
                waits = [(S.s_kev, kv_k_all), (S.s_vev, kv_v_all)]
                if hh % 8 == 0:
                    waits += P.bank_wait(bank)
                def mms(t, ps=ps, hh=hh):
                    last = None
                    for nt in range(NT):
                        last = t.matmul(ps,
                                        KTM[:, nt * 512 + 32 * hh: nt * 512 + 32 * hh + 32],
                                        VPAD[:, nt * 528 + 33 * hh: nt * 528 + 33 * hh + 33],
                                        start=(nt == 0), stop=(nt == NT - 1))
                    return last
                P.op("tensor", mms, waits=waits, inc=S.s_smm)
                if hh % 8 == 7:
                    need = S.s_smm.n
                    gh0 = 16 * g + (hh - 7)
                    def sev(v, bsel=bsel, gh0=gh0):
                        return v.tensor_copy(SFLAT[0:32, gh0 * 33:(gh0 + 8) * 33],
                                             PB[0:32, bsel * 512: bsel * 512 + 264])
                    P.op("vector", sev, waits=[(S.s_smm, need)], inc=S.s_sf)
                    P.bank_free(bank, S.s_sf)

        wq_done = [None] * JT
        wq_done[0] = dma(S.s_wq0, WQ(0), wq_d[0])
        wq_done[1] = dma(S.s_wq1, WQ(1), wq_d[1])

        NQ = JT * 4
        qslot_of = lambda g: g % 2
        nslot_of = lambda g: g % 2
        def QS(g):
            k = g % 4
            return QSB[:, (k % 2) * 512:(k % 2 + 1) * 512] if k < 2 \
                else GS[:, (k - 2) * 512:(k - 1) * 512]

        def emit_snb(jt):
            sf_need = (4 * jt + 3) // 8 + 1
            snb_slot = jt % 2
            for r in range(4):
                gh = 4 * jt + r
                waits = [(S.s_sf, sf_need), consts_done]
                if r == 0:
                    waits += P.bank_wait(("PC", 0))
                def bnb(t, r=r, gh=gh):
                    return t.matmul(PC[:, 32 * r: 32 * r + 32],
                                    ERS[0:32, r * 128:(r + 1) * 128],
                                    SFLAT[0:32, gh * 33: gh * 33 + 32],
                                    start=True, stop=True)
                P.op("tensor", bnb, waits=waits, inc=S.s_nbm)
                def bkw(t, r=r, gh=gh):
                    return t.matmul(PC[:, 128 + r: 129 + r],
                                    ERS[0:32, r * 128:(r + 1) * 128],
                                    SFLAT[0:32, gh * 33 + 32: gh * 33 + 33],
                                    start=True, stop=True)
                P.op("tensor", bkw, inc=S.s_nbm)
            nb_need = S.s_nbm.n
            def snb_ev(v, snb_slot=snb_slot):
                return v.tensor_copy(SNB[:, snb_slot * 132:(snb_slot + 1) * 132],
                                     PC[:, 0:132])
            P.op("vector", snb_ev,
                 waits=[(S.s_nbm, nb_need)] + P.slot_wait(("SNB", snb_slot)),
                 inc=S.s_nbe)
            P.bank_free(("PC", 0), S.s_nbe)
            return S.s_nbe.n

        nbe_need = [0] * JT
        qmm_need = [0] * NQ
        qrl_need = [0] * NQ
        nmm_need = [0] * NQ
        dmm_need = [0] * NQ
        dev_need = [0] * NQ
        bcm_need = [0] * NQ
        bce_need = [0] * NQ

        def emit_q(g):
            jt, nc4 = g // 4, g % 4
            qs = qslot_of(g)
            ps_q = PA[:, qs * 512:(qs + 1) * 512]
            waits = [wq_done[jt]] + P.bank_wait(("PA", qs))
            def mmq(t, ps=ps_q, jt=jt, nc4=nc4):
                last = None
                for cc in range(CT):
                    last = t.matmul(ps,
                                    WQ(jt % 2)[:, cc * 128:(cc + 1) * 128],
                                    STREAM[:, cc * 2048 + nc4 * 512: cc * 2048 + (nc4 + 1) * 512],
                                    start=(cc == 0), stop=(cc == CT - 1))
                return last
            P.op("tensor", mmq, waits=waits, inc=S.s_qmm)
            qmm_need[g] = S.s_qmm.n
            P.op("scalar", lambda s, ps=ps_q, g=g: s.activation(
                    QS(g), ps, AF.Relu),
                 waits=[(S.s_qmm, qmm_need[g])] + P.slot_wait(("QSB", g % 4)),
                 inc=S.s_qrl)
            P.bank_free(("PA", qs), S.s_qrl)
            qrl_need[g] = S.s_qrl.n

        def emit_numden(g):
            jt = g // 4
            qs = qslot_of(g)
            ns = nslot_of(g)
            snb_slot = jt % 2
            def mmn(t, g=g, ns=ns, snb_slot=snb_slot):
                return t.matmul(PB[:, ns * 512:(ns + 1) * 512],
                                SNB[:, snb_slot * 132: snb_slot * 132 + 128],
                                QS(g),
                                start=True, stop=True)
            P.op("tensor", mmn,
                 waits=[(S.s_qrl, qrl_need[g]), (S.s_nbe, nbe_need[jt])]
                       + P.bank_wait(("PB", ns)), inc=S.s_nmm)
            nmm_need[g] = S.s_nmm.n
            def mmd(t, g=g, snb_slot=snb_slot):
                return t.matmul(PC[0:4, 0:512],
                                SNB[:, snb_slot * 132 + 128: snb_slot * 132 + 132],
                                QS(g),
                                start=True, stop=True)
            P.op("tensor", mmd, waits=P.bank_wait(("PC", 0)), inc=S.s_dmm)
            dmm_need[g] = S.s_dmm.n
            P.slot_free(("QSB", g % 4), S.s_dmm)
            P.op("vector", lambda v: v.tensor_single_scalar(
                    DENS[:], PC[0:4, 0:512], EPS_ATTN, ALU.add),
                 waits=[(S.s_dmm, dmm_need[g])] + P.slot_wait("DENS"))
            def drec(v):
                with nc.allow_low_precision(reason="bf16 attn denominator recip"):
                    return v.reciprocal(DENS[:], DENS[:])
            P.op("vector", drec, inc=S.s_dev)
            P.bank_free(("PC", 0), S.s_dev)
            dev_need[g] = S.s_dev.n

        def emit_bc(g):
            jt, nc4 = g // 4, g % 4
            ns = nslot_of(g)
            def mmb(t):
                return t.matmul(PC[:, 512:1024], FBS[:], DENS[:],
                                start=True, stop=True)
            P.op("tensor", mmb,
                 waits=[(S.s_dev, dev_need[g])] + P.bank_wait(("PC", 1)), inc=S.s_bcm)
            bcm_need[g] = S.s_bcm.n
            P.slot_free("DENS", S.s_bcm)
            P.op("vector", lambda v: v.tensor_copy(DENB[:], PC[:, 512:1024]),
                 waits=[(S.s_bcm, bcm_need[g])] + P.slot_wait("DENB"), inc=S.s_bce)
            P.bank_free(("PC", 1), S.s_bce)
            bce_need[g] = S.s_bce.n
            def att(v, jt=jt, nc4=nc4, ns=ns):
                return v.tensor_tensor(
                    ATTN(jt)[:, nc4 * 512:(nc4 + 1) * 512],
                    DENB[:], PB[:, ns * 512:(ns + 1) * 512], ALU.mult)
            P.op("vector", att, waits=[(S.s_bce, bce_need[g]), (S.s_nmm, nmm_need[g])],
                 inc=S.s_att)
            P.bank_free(("PB", ns), S.s_att)
            P.slot_free("DENB", S.s_att)

        for g in range(NQ):
            jt = g // 4
            if g == 0:
                nbe_need[0] = emit_snb(0)
            if g % 4 == 3 and jt + 1 < JT:
                nbe_need[jt + 1] = emit_snb(jt + 1)
            if g >= 4:
                emit_bc(g - 4)
            if g >= 3:
                emit_numden(g - 3)
            emit_q(g)
            if g % 4 == 3:
                P.slot_free(("WQ", jt % 2), S.s_qrl)
                if jt + 2 < JT:
                    wq_done[jt + 2] = dma(S.s_wq0 if jt % 2 == 0 else S.s_wq1,
                                          WQ(jt % 2), wq_d[jt + 2],
                                          extra_waits=P.slot_wait(("WQ", jt % 2)))
        for g in range(NQ - 3, NQ):
            emit_bc(g - 1)
            emit_numden(g)
        emit_bc(NQ - 1)
        attn_all = S.s_att.n

        if debug:
            P.op("gpsimd", lambda g: g.dma_start(out=dbg["attn"], in_=BUF2[:, 0:CT * 2048]),
                 waits=[(S.s_att, attn_all)], inc=S.s_dbg, amt=16)

        wp_done = [None] * CT
        wp_done[0] = dma(S.s_wp0, WP(0), wp_d[0],
                         extra_waits=[(S.s_kev, S.s_kev.n), (S.s_vev, S.s_vev.n)])
        wp_done[1] = dma(S.s_wp1, WP(1), wp_d[1],
                         extra_waits=[(S.s_kev, S.s_kev.n), (S.s_vev, S.s_vev.n)])

        _tr_idx = [0]
        def emit_yinit_nts(half, nts, ap_val):
            for nt in nts:
                ntl = nt - half * NTH
                for ccg in range(4):
                    slot = _tr_idx[0] % 2
                    bank = ("PD", slot)
                    waits = [(S.s_ap, ap_val), consts_done] + P.bank_wait(bank)
                    def trp4(t, slot=slot, ccg=ccg, nt=nt):
                        last = None
                        for k in range(4):
                            cc = ccg * 4 + k
                            last = t.transpose(
                                PD[:, slot * 1024 + k * 128: slot * 1024 + (k + 1) * 128],
                                STREAM[:, cc * 2048 + nt * 128: cc * 2048 + (nt + 1) * 128],
                                IDN[:])
                        return last
                    P.op("tensor", trp4, waits=waits, inc=S.s_tr)
                    tr_need = S.s_tr.n
                    def trev(v, slot=slot, half=half, ntl=ntl, ccg=ccg):
                        return v.tensor_copy(
                            YACC(half, ntl)[:, ccg * 512:(ccg + 1) * 512],
                            PD[:, slot * 1024: slot * 1024 + 512])
                    P.op("vector", trev, waits=[(S.s_tr, tr_need)], inc=S.s_tre)
                    P.bank_free(bank, S.s_tre)
                    _tr_idx[0] += 1

        PROJ_PS = [PA[0:1, 0:512], PA[0:1, 512:1024], PB[0:1, 0:512], PB[0:1, 512:1024]]
        PROJ_PSF = [PA[:, 0:512], PA[:, 512:1024], PB[:, 0:512], PB[:, 512:1024]]
        PROJ_BANK = [("PA", 0), ("PA", 1), ("PB", 0), ("PB", 1)]
        for cot in range(CT):
            for q in range(4):
                ps = PROJ_PSF[q]
                waits = [wp_done[cot], (S.s_att, attn_all)] + P.bank_wait(PROJ_BANK[q])
                def mmp(t, ps=ps, cot=cot, q=q):
                    last = None
                    for cc in range(CT):
                        last = t.matmul(ps,
                                        WP(cot % 2)[:, cc * 128:(cc + 1) * 128],
                                        ATTN(cc)[:, q * 512:(q + 1) * 512],
                                        start=(cc == 0), stop=(cc == CT - 1))
                    return last
                P.op("tensor", mmp, waits=waits, inc=S.s_pmm)
                pmm_need = S.s_pmm.n
                def pev(v, ps=ps, cot=cot, q=q):
                    dst = STREAM[:, cot * 2048 + q * 512: cot * 2048 + (q + 1) * 512]
                    return v.scalar_tensor_tensor(dst, ps, BSB[:, cot:cot + 1], dst,
                                                  ALU.add, ALU.add)
                P.op("vector", pev, waits=[(S.s_pmm, pmm_need), (S.s_ld, bp_done)],
                     inc=S.s_pev)
                P.bank_free(PROJ_BANK[q], S.s_pev)
            P.slot_free(("WP", cot % 2), S.s_pev)
            if cot + 2 < CT:
                wp_done[cot + 2] = dma(S.s_wp0 if cot % 2 == 0 else S.s_wp1,
                                       WP(cot % 2), wp_d[cot + 2],
                                       extra_waits=P.slot_wait(("WP", cot % 2)))
        pev_all = S.s_pev.n

        SQPS = [PC[0:1, 0:512], PC[0:1, 512:1024], PDF[0:1, 0:512], PDF[0:1, 512:1024]]
        SQBANK = [("PC", 0), ("PC", 1), ("PD", 0), ("PD", 1)]
        sq_done = [0] * CT
        for cc in range(CT):
            def sqr(v, cc=cc):
                sl = STREAM[:, cc * 2048:(cc + 1) * 2048]
                return v.tensor_tensor(SQA(cc), sl, sl, ALU.mult)
            waits = [(S.s_pev, 4 * (cc + 1))]
            if cc >= 8:
                waits.append((S.s_sqc, cc - 7))
            P.op("vector", sqr, waits=waits, inc=S.s_sq)
            sq_done[cc] = S.s_sq.n
        for cc in range(CT):
            waits = [(S.s_pev, pev_all)]
            if cc == 0:
                for q in range(4):
                    waits += P.bank_wait(PROJ_BANK[q])
            def mmsx(t, cc=cc):
                last = None
                for q in range(4):
                    last = t.matmul(PROJ_PS[q], ONC[:],
                                    STREAM[:, cc * 2048 + q * 512: cc * 2048 + (q + 1) * 512],
                                    start=(cc == 0), stop=(cc == CT - 1))
                return last
            P.op("tensor", mmsx, waits=waits,
                 inc=(S.s_sxm if cc == CT - 1 else None))
            waits = [(S.s_sq, sq_done[cc])]
            if cc == 0:
                for q in range(4):
                    waits += P.bank_wait(SQBANK[q])
            def mmsq(t, cc=cc):
                last = None
                for q in range(4):
                    last = t.matmul(SQPS[q], ONC[:],
                                    SQA(cc)[:, q * 512:(q + 1) * 512],
                                    start=(cc == 0), stop=(cc == CT - 1))
                return last
            inc = S.s_sqc if cc < CT - 1 else S.s_sqm
            P.op("tensor", mmsq, waits=waits, inc=inc)
        sx_need = S.s_sxm.n
        sqm_need = S.s_sqm.n
        for q in range(4):
            P.op("vector", lambda v, q=q: v.tensor_single_scalar(
                    MROW[0:1, q * 512:(q + 1) * 512], PROJ_PS[q], 1.0 / C, ALU.mult),
                 waits=[(S.s_sxm, sx_need)], inc=S.s_row)
            P.bank_free(PROJ_BANK[q], S.s_row)
        MSQV = OUT_SB[0:1, :]
        row_need0 = S.s_row.n
        P.op("vector", lambda v: v.tensor_tensor(MSQV, MROW[0:1, :], MROW[0:1, :], ALU.mult),
             waits=[(S.s_row, row_need0), (S.s_sqm, sqm_need)])
        for q in range(4):
            def vvar(v, q=q):
                return v.scalar_tensor_tensor(MSQV[0:1, q * 512:(q + 1) * 512],
                                              SQPS[q], 1.0 / C,
                                              MSQV[0:1, q * 512:(q + 1) * 512],
                                              ALU.mult, ALU.subtract)
            P.op("vector", vvar, inc=S.s_row)
            P.bank_free(SQBANK[q], S.s_row)
        P.op("vector", lambda v: v.tensor_single_scalar(MSQV, MSQV, EPS_LN, ALU.add),
             inc=S.s_row)
        row_need = S.s_row.n
        P.op("scalar", lambda s: s.activation(RROW[0:1, :], MSQV, AF.Sqrt),
             waits=[(S.s_row, row_need)], inc=S.s_sq)
        sd_need = S.s_sq.n
        def rrec(v):
            with nc.allow_low_precision(reason="bf16 rstd recip"):
                return v.reciprocal(RROW[0:1, :], RROW[0:1, :])
        P.op("vector", rrec, waits=[(S.s_sq, sd_need)], inc=S.s_row)
        rowm_need = S.s_row.n

        BCPAIRS = [(("PA", 0), ("PA", 1)), (("PB", 0), ("PB", 1)),
                   (("PC", 0), ("PC", 1)), (("PC", 0), ("PC", 1))]
        BCPS = [(PA[:, 0:512], PA[:, 512:1024]), (PB[:, 0:512], PB[:, 512:1024]),
                (PC[:, 0:512], PC[:, 512:1024]), (PC[:, 0:512], PC[:, 512:1024])]
        MBC = DENB[:]
        RBC = QSB[:, 0:512]
        for q in range(4):
            bm, br = BCPAIRS[q]
            pm, pr = BCPS[q]
            def mmbm(t, q=q, pm=pm):
                return t.matmul(pm, ONR[:], MROW[0:1, q * 512:(q + 1) * 512],
                                start=True, stop=True)
            P.op("tensor", mmbm, waits=[(S.s_row, rowm_need)] + P.bank_wait(bm),
                 inc=S.s_bc2)
            bcm_v = S.s_bc2.n
            def mmbr(t, q=q, pr=pr):
                return t.matmul(pr, ONR[:], RROW[0:1, q * 512:(q + 1) * 512],
                                start=True, stop=True)
            P.op("tensor", mmbr, waits=P.bank_wait(br), inc=S.s_bc2)
            bcr_v = S.s_bc2.n
            P.op("scalar", lambda s, pm=pm: s.copy(MBC, pm),
                 waits=[(S.s_bc2, bcm_v)] + P.slot_wait("MBC"), inc=S.s_sq)
            P.bank_free(bm, S.s_sq)
            P.op("scalar", lambda s, pr=pr: s.copy(RBC, pr),
                 waits=[(S.s_bc2, bcr_v)] + P.slot_wait("RBC"), inc=S.s_sq)
            P.bank_free(br, S.s_sq)
            if q in (1, 2):
                emit_yinit_nts(0, range(4 * (q - 1), 4 * q), 16 * q)
            bce_v = S.s_sq.n
            for cc in range(CT):
                def app1(v, cc=cc, q=q):
                    return v.tensor_tensor(TAPP,
                                           STREAM[:, cc * 2048 + q * 512: cc * 2048 + (q + 1) * 512],
                                           MBC, ALU.subtract)
                P.op("vector", app1, waits=[(S.s_sq, bce_v)])
                def app2(v, cc=cc, q=q):
                    return v.tensor_tensor(
                        STREAM[:, cc * 2048 + q * 512: cc * 2048 + (q + 1) * 512],
                        TAPP, RBC, ALU.mult)
                P.op("vector", app2, inc=S.s_ap)
            P.slot_free("MBC", S.s_ap)
            P.slot_free("RBC", S.s_ap)
        ap_all = S.s_ap.n

        if debug:
            P.op("gpsimd", lambda g: g.dma_start(out=dbg["x1"], in_=STREAM[:, 0:CT * 2048]),
                 waits=[(S.s_ap, ap_all)], inc=S.s_dbg, amt=16)

        rmm_vals = [0, 0]
        router_state = {}
        def emit_router_scores():
            for half_b in range(2):
                bank = ("PC", half_b)
                waits0 = [(S.s_ap, ap_all), consts_done] + P.bank_wait(bank)
                def mmr_all(t, half_b=half_b, waits_done=None):
                    last = None
                    for ntl in range(8):
                        nt = half_b * 8 + ntl
                        for cc in range(CT):
                            last = t.matmul(
                                PC[0:128, half_b * 512 + ntl * 8: half_b * 512 + ntl * 8 + 8],
                                STREAM[:, cc * 2048 + nt * 128: cc * 2048 + (nt + 1) * 128],
                                WRS[:, cc * 8:(cc + 1) * 8],
                                start=(cc == 0), stop=(cc == CT - 1))
                    return last
                P.op("tensor", mmr_all, waits=waits0, inc=S.s_rmm)
                rmm_vals[half_b] = S.s_rmm.n
            for half_b in range(2):
                P.op("vector", lambda v, half_b=half_b: v.tensor_copy(
                        SCSB[:, half_b * 64:(half_b + 1) * 64],
                        PC[0:128, half_b * 512: half_b * 512 + 64]),
                     waits=[(S.s_rmm, rmm_vals[half_b])], inc=S.s_re)
                P.bank_free(("PC", half_b), S.s_re)
            sc_done = S.s_re.n
            router_state["sc_done"] = sc_done

        def emit_router_gates(nts):
            sc_done = router_state["sc_done"]
            for nt in nts:
                sc = SCSB[:, nt * 8:(nt + 1) * 8]
                def vch(fn, extra=()):
                    P.op("vector", fn, waits=list(extra) + [(S.s_vch, S.s_vch.n)], inc=S.s_vch)
                vch(lambda v, sc=sc: v.reduce_max(SM1[:], sc, axis=AX.X, negate=True),
                    [(S.s_re, sc_done)])
                re_need = S.s_vch.n
                P.op("scalar", lambda s, sc=sc: s.activation(SE8[:], sc, AF.Exp, bias=SM1[:]),
                     waits=[(S.s_vch, re_need)], inc=S.s_sq)
                e_need = S.s_sq.n
                vch(lambda v: v.reduce_max(SM2[:], SE8[:], axis=AX.X), [(S.s_sq, e_need)])
                vch(lambda v: v.tensor_scalar(ST8[:], SE8[:], SM2[:], None, ALU.is_ge))
                vch(lambda v: v.scalar_tensor_tensor(ST8[:], ST8[:], -2.0, SE8[:],
                                                     ALU.mult, ALU.add))
                vch(lambda v: v.reduce_max(SM3[:], ST8[:], axis=AX.X))
                vch(lambda v: v.tensor_scalar(SL8[:], SE8[:], SM3[:], None, ALU.is_ge))
                vch(lambda v: v.tensor_tensor(SM2[:], SM2[:], SM3[:], ALU.add))
                vch(lambda v: v.reciprocal(SM2[:], SM2[:]))
                vch(lambda v: v.tensor_tensor(ST8[:], SE8[:], SL8[:], ALU.mult))
                vch(lambda v, nt=nt: v.tensor_scalar(
                        WSB[:, nt * 8:(nt + 1) * 8], ST8[:], SM2[:], None, ALU.mult))

            router_state["done"] = S.s_vch.n
            if debug:
                P.op("gpsimd", lambda g: g.dma_start(out=dbg["w"], in_=WSB[:]),
                     waits=[(S.s_vch, router_state["done"])], inc=S.s_dbg, amt=16)

        wg_done = {}
        wu_done = {}
        wd_done = {}

        def emit_wgu(eh, ft):
            half, ei = eh // NE, eh % NE
            gi = eh * NF + ft
            waits = []
            if gi >= 2:
                waits.append((S.s_sil, 2 * gi - 2))
            else:
                waits.append((S.s_pev, pev_all))
            wg_done[eh, ft] = dma(S.s_wg0 if ft % 2 == 0 else S.s_wg1,
                                  WGB(ft % 2), wg_d[ei, ft], extra_waits=waits)
            waits = []
            if gi >= 2:
                waits.append((S.s_h, 2 * gi - 2))
            else:
                waits.append((S.s_kev, 64))
            wu_done[eh, ft] = dma(S.s_wu0 if ft % 2 == 0 else S.s_wu1,
                                  WUB(ft % 2), wu_d[ei, ft], extra_waits=waits)

        def emit_wd(eh, cq):
            half, ei = eh // NE, eh % NE
            di = eh * NCQ + cq
            waits = []
            if di >= 2:
                waits.append((S.s_yev, 8 * di - 8))
            else:
                waits.append((S.s_qrl, 4 * JT))
                waits.append((S.s_sqm, sqm_need))
                waits.append((S.s_vev, 65))
            wd_done[eh, cq] = dma2(S.s_wd0 if cq % 2 == 0 else S.s_wd1,
                                   WDB(cq % 2), wd_d[ei, cq], extra_waits=waits)

        for eh in range(2 * NE):
            if eh == 0:
                emit_wgu(0, 0)
                emit_wgu(0, 1)
            for ft in range(2, NF):
                emit_wgu(eh, ft)
            emit_wd(eh, 0)
            emit_wd(eh, 1)
            if eh + 1 < 2 * NE:
                emit_wgu(eh + 1, 0)
                emit_wgu(eh + 1, 1)
            emit_wd(eh, 2)
            emit_wd(eh, 3)

        ydone = np.zeros((2, NTH), dtype=np.int64)
        cact_base = [0, 0]


        def emit_half_experts(half, c_half=None, router=False, yinit1=False):
            for ei in range(NE):
                eh = half * NE + ei
                for ft in range(NF):
                    if yinit1 and ei == 0 and ft < NTH:
                        emit_yinit_nts(1, [NTH + ft], ap_all)
                    if c_half is not None and ei == 0 and ft % 2 == 0:
                        emit_c_tile(c_half, ft // 2)
                    if router and ei == 0 and ft == 2:
                        emit_router_scores()
                        emit_router_gates([0, 1, 2])
                    elif router and ei == 0 and 2 < ft < NF:
                        emit_router_gates([ft])
                    waits = ([wg_done[eh, ft]]
                             + P.bank_wait(("PA", 0)) + P.bank_wait(("PA", 1)))
                    if eh == 0:
                        waits.append((S.s_ap, 32))
                    umerge = {}
                    for (s, v) in ([wu_done[eh, ft]] + P.bank_wait(("PB", 0))
                                   + P.bank_wait(("PB", 1))):
                        if s is not None and v > 0:
                            if umerge.get(id(s), (None, -1))[1] < v:
                                umerge[id(s)] = (s, v)
                    u_waits = list(umerge.values())
                    S.s_gmm.n += 2
                    gneed = (S.s_gmm.n - 1, S.s_gmm.n)
                    S.s_umm.n += 2
                    uneed = (S.s_umm.n - 1, S.s_umm.n)
                    def mmgu(t, ft=ft, half=half, gsem=S.s_gmm, usem=S.s_umm,
                             u_waits=u_waits):
                        lasts = [None, None]
                        for dc in range(CT):
                            for q in range(2):
                                lasts[q] = t.matmul(
                                    PA[:, q * 512:(q + 1) * 512],
                                    WGB(ft % 2)[:, dc * 128:(dc + 1) * 128],
                                    STREAM[:, dc * 2048 + half * 1024 + q * 512:
                                           dc * 2048 + half * 1024 + (q + 1) * 512],
                                    start=(dc == 0), stop=(dc == CT - 1))
                        lasts[0].then_inc(gsem.h, 1)
                        lasts[1].then_inc(gsem.h, 1)
                        for (s, v) in u_waits:
                            t.wait_ge(s.h, v)
                        for dc in range(CT):
                            for q in range(2):
                                lasts[q] = t.matmul(
                                    PB[:, q * 512:(q + 1) * 512],
                                    WUB(ft % 2)[:, dc * 128:(dc + 1) * 128],
                                    STREAM[:, dc * 2048 + half * 1024 + q * 512:
                                           dc * 2048 + half * 1024 + (q + 1) * 512],
                                    start=(dc == 0), stop=(dc == CT - 1))
                        lasts[0].then_inc(usem.h, 1)
                        lasts[1].then_inc(usem.h, 1)
                        return None
                    P.op("tensor", mmgu, waits=waits)
                    sil_needs = []
                    for q in range(2):
                        def sil(s, q=q):
                            return s.activation(GS[:, q * 512:(q + 1) * 512],
                                                PA[:, q * 512:(q + 1) * 512], AF.Silu)
                        P.op("scalar", sil,
                             waits=[(S.s_gmm, gneed[q])] + P.slot_wait(("GS", q)),
                             inc=S.s_sil)
                        P.bank_free(("PA", q), S.s_sil)
                        sil_needs.append(S.s_sil.n)
                    for q in range(2):
                        hwaits = [(S.s_sil, sil_needs[q]), (S.s_umm, uneed[q])]
                        if half == 1:
                            hwaits.append((S.s_cact, cact_base[0] + ft // 2 + 1))
                        def hev(v, half=half, ft=ft, q=q):
                            return v.tensor_tensor(
                                HB(half, ft)[:, q * 512:(q + 1) * 512],
                                GS[:, q * 512:(q + 1) * 512],
                                PB[:, q * 512:(q + 1) * 512], ALU.mult)
                        P.op("vector", hev, waits=hwaits, inc=S.s_h)
                        P.bank_free(("PB", q), S.s_h)
                        P.slot_free(("GS", q), S.s_h)
                h_all = S.s_h.n
                YPS = [PC[:, 0:512], PC[:, 512:1024], PDF[0:128, 0:512], PDF[0:128, 512:1024]]
                YBK = [("PC", 0), ("PC", 1), ("PD", 0), ("PD", 1)]
                for cq in range(NCQ):
                    for ntl0 in range(0, NTH, 2):
                        k = S.s_ymm.n
                        waits = [wd_done[eh, cq], (S.s_h, h_all)]
                        waits += P.bank_wait(YBK[k % 4])
                        j1_waits = [(s, v) for (s, v) in P.bank_wait(YBK[(k + 1) % 4])
                                    if s is not None and v > 0]
                        S.s_ymm.n += 2
                        yneed = (S.s_ymm.n - 1, S.s_ymm.n)
                        def mmy2(t, half=half, ntl0=ntl0, cq=cq, k=k, sem=S.s_ymm,
                                 j1_waits=j1_waits):
                            for j in range(2):
                                if j == 1:
                                    for (s, v) in j1_waits:
                                        t.wait_ge(s.h, v)
                                ntl = ntl0 + j
                                ps = YPS[(k + j) % 4]
                                last = None
                                for fc in range(NF):
                                    last = t.matmul(ps,
                                                    HB(half, fc)[:, ntl * 128:(ntl + 1) * 128],
                                                    WDB(cq % 2)[:, fc * 512:(fc + 1) * 512],
                                                    start=(fc == 0), stop=(fc == NF - 1))
                                last.then_inc(sem.h, 1)
                            return None
                        P.op("tensor", mmy2, waits=waits)
                        for j in range(2):
                            ntl = ntl0 + j
                            ps_y = YPS[(k + j) % 4]
                            nt = half * NTH + ntl
                            ywaits = [(S.s_ymm, yneed[j]), (S.s_vch, router_state["done"]),
                                      (S.s_tre, (half + 1) * NTH * 4)]
                            def yev(v, ps=ps_y, half=half, ntl=ntl, cq=cq, nt=nt, ei=ei):
                                dst = YACC(half, ntl)[:, cq * 512:(cq + 1) * 512]
                                return v.scalar_tensor_tensor(
                                    dst, ps, WSB[:, nt * 8 + ei: nt * 8 + ei + 1], dst,
                                    ALU.mult, ALU.add)
                            P.op("vector", yev, waits=ywaits, inc=S.s_yev)
                            P.bank_free(YBK[(k + j) % 4], S.s_yev)
                            if ei == NE - 1 and cq == NCQ - 1:
                                ydone[half, ntl] = S.s_yev.n

        OUT2 = SCR[:, 0:4096].bitcast(F32)
        def emit_c_tile(half, ntl):
                nt = half * NTH + ntl
                k = nt
                ytile = YACC(half, ntl)
                OUTB = OUT_SB[:] if (half == 0 or k % 2 == 0) else OUT2
                def cch(fn, extra=()):
                    P.op("vector", fn, waits=list(extra) + [(S.s_vch, S.s_vch.n)],
                         inc=S.s_vch)
                w0 = [(S.s_yev, int(ydone[half, ntl]))]
                if half == 0 and k > 0:
                    w0.append((S.s_cact, k))
                cch(lambda v, ytile=ytile: v.reduce_sum(CS[:], ytile, axis=AX.X), w0)
                cch(lambda v: v.tensor_single_scalar(CM[:], CS[:], 1.0 / C, ALU.mult))
                sqw = [(S.s_yev, int(ydone[half, ntl]))]
                if half == 0:
                    if k >= 1:
                        sqw.append((S.s_out, 16 * k))
                else:
                    prev = k - 2 if k >= 10 else (7 if k % 2 == 0 else -1)
                    if prev >= 0:
                        sqw.append((S.s_out, 16 * (prev + 1)))
                P.op("scalar", lambda s, ytile=ytile, OUTB=OUTB: s.activation(
                        OUTB, ytile, AF.Square), waits=sqw, inc=S.s_csq)
                csq_need = S.s_csq.n
                cch(lambda v, OUTB=OUTB: v.reduce_sum(CS[:], OUTB, axis=AX.X),
                    [(S.s_csq, csq_need)])
                cch(lambda v: v.tensor_tensor(CV[:], CM[:], CM[:], ALU.mult))
                cch(lambda v: v.scalar_tensor_tensor(CV[:], CS[:], 1.0 / C, CV[:],
                                                     ALU.mult, ALU.subtract))
                cch(lambda v: v.tensor_single_scalar(CV[:], CV[:], EPS_LN, ALU.add))
                cr2b = S.s_vch.n
                P.op("scalar", lambda s: s.activation(CS[:], CV[:], AF.Sqrt),
                     waits=[(S.s_vch, cr2b)], inc=S.s_csq)
                csq2 = S.s_csq.n
                cch(lambda v: v.reciprocal(CS[:], CS[:]), [(S.s_csq, csq2)])
                P.op("vector", lambda v, ytile=ytile, OUTB=OUTB: v.tensor_scalar(
                        OUTB, ytile, CM[:], CS[:], ALU.subtract, ALU.mult),
                     waits=[(S.s_vch, S.s_vch.n)], inc=S.s_cact)
                cact_need = S.s_cact.n
                def outdma(g, nt=nt, OUTB=OUTB):
                    return g.dma_start(out=out_d[nt * 128:(nt + 1) * 128, :],
                                       in_=OUTB)
                P.op("sync", outdma, waits=[(S.s_cact, cact_need)], inc=S.s_out, amt=16)

        emit_half_experts(0, router=True)
        cact_base[0] = S.s_cact.n
        emit_half_experts(1, c_half=0, yinit1=True)
        for ntl in range(NTH):
            emit_c_tile(1, ntl)

        final_out = S.s_out.n
        final_dbg = S.s_dbg.n
        def fin(g):
            g.wait_ge(S.s_out.h, final_out)
            if debug and final_dbg:
                g.wait_ge(S.s_dbg.h, final_dbg)
            return None
        P.op("sync", fin)

        with nc.Block() as block:
            @block.tensor
            def _(eng):
                P.run("tensor", eng)

            @block.vector
            def _(eng):
                P.run("vector", eng)

            @block.scalar
            def _(eng):
                P.run("scalar", eng)

            @block.gpsimd
            def _(eng):
                P.run("gpsimd", eng)

            @block.sync
            def _(eng):
                P.run("sync", eng)

    return nc



def _pack(np_inputs):
    f32 = np.float32
    wqkv = np.asarray(np_inputs["L0_attn_w_qkv"], f32)
    wproj = np.asarray(np_inputs["L0_attn_w_proj"], f32)
    bproj = np.asarray(np_inputs["L0_attn_b_proj"], f32)
    Wr = np.asarray(np_inputs["L0_ffn_Wr"], f32)
    Wg = np.asarray(np_inputs["L0_ffn_Wg"], f32)
    Wu = np.asarray(np_inputs["L0_ffn_Wu"], f32)
    Wd = np.asarray(np_inputs["L0_ffn_Wd"], f32)

    d = {}
    d["wq"] = np.ascontiguousarray(
        wqkv[0:2048].reshape(16, 128, 16, 128).transpose(0, 3, 2, 1)).reshape(16, 128, 2048)
    d["wk"] = np.ascontiguousarray(
        wqkv[2048:4096].reshape(4, 512, 16, 128).transpose(0, 3, 2, 1)).reshape(4, 128, 8192)
    d["wv"] = np.ascontiguousarray(
        wqkv[4096:6144].reshape(4, 512, 16, 128).transpose(0, 3, 2, 1)).reshape(4, 128, 8192)
    d["wp"] = np.ascontiguousarray(
        wproj.reshape(16, 128, 16, 128).transpose(0, 3, 2, 1)).reshape(16, 128, 2048)
    d["bp"] = np.ascontiguousarray(bproj.reshape(16, 128).T)
    d["wr"] = np.ascontiguousarray(Wr.reshape(16, 128, 8).transpose(1, 0, 2)).reshape(128, 128)
    d["wg"] = np.ascontiguousarray(
        Wg.reshape(8, 16, 128, 16, 128).transpose(0, 3, 2, 1, 4)).reshape(8, 16, 128, 2048)
    d["wu"] = np.ascontiguousarray(
        Wu.reshape(8, 16, 128, 16, 128).transpose(0, 3, 2, 1, 4)).reshape(8, 16, 128, 2048)
    d["wd"] = np.ascontiguousarray(
        Wd.reshape(8, 16, 128, 4, 512).transpose(0, 3, 2, 1, 4)).reshape(8, 4, 128, 8192)
    d["id_in"] = np.eye(128, dtype=f32)
    er = np.zeros((32, 512), f32)
    for r in range(4):
        for dd in range(32):
            er[dd, r * 128 + 32 * r + dd] = 1.0
    d["er_in"] = er
    fb = np.zeros((4, 128), f32)
    for r in range(4):
        fb[r, 32 * r:32 * r + 32] = 1.0
    d["fb_in"] = fb
    d["onr_in"] = np.ones((1, 128), f32)
    d["onc_in"] = np.ones((128, 1), f32)
    return d


def kernel(L0_attn_x, L0_attn_w_qkv, L0_attn_w_proj, L0_attn_b_proj,
           L0_ffn_Wr, L0_ffn_Wg, L0_ffn_Wu, L0_ffn_Wd):
    x = np.asarray(L0_attn_x, np.float32)
    shared = _pack({
        "L0_attn_w_qkv": L0_attn_w_qkv, "L0_attn_w_proj": L0_attn_w_proj,
        "L0_attn_b_proj": L0_attn_b_proj, "L0_ffn_Wr": L0_ffn_Wr,
        "L0_ffn_Wg": L0_ffn_Wg, "L0_ffn_Wu": L0_ffn_Wu, "L0_ffn_Wd": L0_ffn_Wd,
    })
    nc = build_program(debug=False)
    in_maps = []
    for b in range(8):
        m = dict(shared)
        m["xT"] = np.ascontiguousarray(x[b].T)
        in_maps.append(m)
    res = run_bass_kernel_spmd(nc, in_maps, core_ids=list(range(8)))
    out = np.stack([np.asarray(res.results[b]["out"], np.float32) for b in range(8)], axis=0)
    return out


# version 21
# speedup vs baseline: 1.1324x; 1.1324x over previous
import numpy as np
from contextlib import ExitStack

import concourse.bass as bass
import concourse.mybir as mybir
from concourse.bass_utils import run_bass_kernel_spmd

F32 = mybir.dt.float32
BF16 = mybir.dt.bfloat16
AF = mybir.ActivationFunctionType
ALU = mybir.AluOpType
AX = mybir.AxisListType

C = 2048
N = 2048
CT = 16
NT = 16
NG = 4
JT = 16
NE = 8
NF = 16
NCQ = 4
NTH = 8
EPS_ATTN = 1e-15
EPS_LN = 1e-5


class Sem:
    def __init__(self, handle):
        self.h = handle
        self.n = 0


class Prog:
    ENGINES = ("tensor", "vector", "scalar", "gpsimd", "sync")

    def __init__(self):
        self.ops = {e: [] for e in self.ENGINES}
        self.bank_guard = {}
        self.slot_guard = {}

    def op(self, eng, fn, waits=(), inc=None, amt=1):
        best = {}
        for (s, v) in waits:
            if s is not None and v > 0:
                if best.get(id(s), (None, -1))[1] < v:
                    best[id(s)] = (s, v)
        waits = list(best.values())
        if inc is not None:
            inc.n += amt
        self.ops[eng].append((waits, fn, inc, amt))

    def bank_wait(self, bank):
        g = self.bank_guard.get(bank)
        return [g] if g else []

    def bank_free(self, bank, sem):
        self.bank_guard[bank] = (sem, sem.n)

    def slot_wait(self, key):
        g = self.slot_guard.get(key)
        return [g] if g else []

    def slot_free(self, key, sem):
        self.slot_guard[key] = (sem, sem.n)

    def run(self, eng_name, eng):
        for waits, fn, inc, amt in self.ops[eng_name]:
            for s, v in waits:
                eng.wait_ge(s.h, v)
            ins = fn(eng)
            if inc is not None:
                if ins is None:
                    ins = eng.nop()
                ins.then_inc(inc.h, amt)


def build_program(debug=False):
    nc = bass.Bass("TRN2", target_bir_lowering=False)

    xT = nc.dram_tensor("xT", [C, N], F32, kind="ExternalInput").ap()
    wq_d = nc.dram_tensor("wq", [JT, 128, 2048], F32, kind="ExternalInput").ap()
    wk_d = nc.dram_tensor("wk", [NG, 128, 8192], F32, kind="ExternalInput").ap()
    wv_d = nc.dram_tensor("wv", [NG, 128, 8192], F32, kind="ExternalInput").ap()
    wp_d = nc.dram_tensor("wp", [CT, 128, 2048], F32, kind="ExternalInput").ap()
    bp_d = nc.dram_tensor("bp", [128, 16], F32, kind="ExternalInput").ap()
    wr_d = nc.dram_tensor("wr", [128, 128], F32, kind="ExternalInput").ap()
    wg_d = nc.dram_tensor("wg", [NE, NF, 128, 2048], F32, kind="ExternalInput").ap()
    wu_d = nc.dram_tensor("wu", [NE, NF, 128, 2048], F32, kind="ExternalInput").ap()
    wd_d = nc.dram_tensor("wd", [NE, NCQ, 128, 8192], F32, kind="ExternalInput").ap()
    id_d = nc.dram_tensor("id_in", [128, 128], F32, kind="ExternalInput").ap()
    er_d = nc.dram_tensor("er_in", [32, 512], F32, kind="ExternalInput").ap()
    fb_d = nc.dram_tensor("fb_in", [4, 128], F32, kind="ExternalInput").ap()
    onr_d = nc.dram_tensor("onr_in", [1, 128], F32, kind="ExternalInput").ap()
    onc_d = nc.dram_tensor("onc_in", [128, 1], F32, kind="ExternalInput").ap()
    out_d = nc.dram_tensor("out", [N, C], F32, kind="ExternalOutput").ap()
    dbg = {}
    if debug:
        dbg["x1"] = nc.dram_tensor("dbg_x1", [128, CT * 2048], F32, kind="ExternalOutput").ap()
        dbg["attn"] = nc.dram_tensor("dbg_attn", [128, CT * 2048], F32, kind="ExternalOutput").ap()
        dbg["w"] = nc.dram_tensor("dbg_w", [128, 128], F32, kind="ExternalOutput").ap()

    with ExitStack() as stack:
        e = stack.enter_context

        STREAM = e(nc.sbuf_tensor("STREAM", [128, CT * 2048], BF16))
        BUF2 = e(nc.sbuf_tensor("BUF2", [128, CT * 2048], BF16))
        SCR = e(nc.sbuf_tensor("SCR", [128, 24576], BF16))
        OUT_SB = e(nc.sbuf_tensor("OUT_SB", [128, 2048], F32))
        GS = e(nc.sbuf_tensor("GS", [128, 1024], BF16))
        QSB = e(nc.sbuf_tensor("QSB", [128, 1024], BF16))
        TAPP = QSB[:, 512:1024]
        SFLAT = e(nc.sbuf_tensor("SFLAT", [32, 2112], BF16))
        SNB = e(nc.sbuf_tensor("SNB", [128, 264], BF16))
        DENS = e(nc.sbuf_tensor("DENS", [4, 512], BF16))
        DENB = e(nc.sbuf_tensor("DENB", [128, 512], BF16))
        WSB = e(nc.sbuf_tensor("WSB", [128, 128], F32))
        SCSB = e(nc.sbuf_tensor("SCSB", [128, 128], F32))
        MROW = e(nc.sbuf_tensor("MROW", [1, 2048], BF16))
        RROW = e(nc.sbuf_tensor("RROW", [1, 2048], BF16))
        BSB = e(nc.sbuf_tensor("BSB", [128, 16], F32))
        WRS = e(nc.sbuf_tensor("WRS", [128, 128], BF16))
        IDN = e(nc.sbuf_tensor("IDN", [128, 128], BF16))
        ERS = e(nc.sbuf_tensor("ERS", [32, 512], BF16))
        FBS = e(nc.sbuf_tensor("FBS", [4, 128], BF16))
        ONR = e(nc.sbuf_tensor("ONR", [1, 128], BF16))
        ONC = e(nc.sbuf_tensor("ONC", [128, 1], BF16))
        SM1 = e(nc.sbuf_tensor("SM1", [128, 1], F32))
        SM2 = e(nc.sbuf_tensor("SM2", [128, 1], F32))
        SM3 = e(nc.sbuf_tensor("SM3", [128, 1], F32))
        SE8 = e(nc.sbuf_tensor("SE8", [128, 8], F32))
        ST8 = e(nc.sbuf_tensor("ST8", [128, 8], F32))
        SL8 = e(nc.sbuf_tensor("SL8", [128, 8], F32))
        CM = e(nc.sbuf_tensor("CM", [128, 1], F32))
        CV = e(nc.sbuf_tensor("CV", [128, 1], F32))
        CS = e(nc.sbuf_tensor("CS", [128, 1], F32))
        CB = e(nc.sbuf_tensor("CB", [128, 1], F32))

        PA = e(nc.psum_tensor("PA", [128, 1024], F32))
        PB = e(nc.psum_tensor("PB", [128, 1024], F32))
        PC = e(nc.psum_tensor("PC", [128, 1024], F32))
        PD = e(nc.psum_tensor("PD", [128, 2048], BF16))
        PDF = PD[:].bitcast(F32)

        sem_names = ["s_x", "s_wkvK", "s_wkvV", "s_wq0", "s_wq1", "s_wp0",
                     "s_wp1", "s_wg0", "s_wg1", "s_wu0", "s_wu1", "s_wd0",
                     "s_wd1", "s_ld", "s_p1", "s_kev", "s_vev", "s_smm", "s_sf",
                     "s_nbm", "s_nbe", "s_qmm", "s_qrl", "s_nmm", "s_dmm",
                     "s_dev", "s_bcm", "s_bce", "s_att", "s_pmm", "s_pev",
                     "s_sxm", "s_sqm", "s_row", "s_bc2", "s_ap", "s_sqc",
                     "s_sq", "s_rmm", "s_re", "s_tr", "s_tre", "s_gmm",
                     "s_sil", "s_umm", "s_h", "s_ymm", "s_yev", "s_csq",
                     "s_cr", "s_cact", "s_out", "s_dbg", "s_vch"]
        SD = {}
        for name in sem_names:
            SD[name] = Sem(e(nc.semaphore(name)))
        S = type("SS", (), SD)

        P = Prog()

        def dma(sem, dst_ap, src_ap, extra_waits=()):
            P.op("gpsimd", lambda g, d=dst_ap, s=src_ap: g.dma_start(out=d, in_=s),
                 waits=extra_waits, inc=sem, amt=16)
            return (sem, sem.n)

        def dma2(sem, dst_ap, src_ap, extra_waits=()):
            n = dst_ap.shape[-1]
            P.op("gpsimd", lambda g, d=dst_ap, s=src_ap: g.dma_start(
                    out=d[:, 0:n // 2], in_=s[:, 0:n // 2]),
                 waits=extra_waits, inc=sem, amt=16)
            P.op("gpsimd", lambda g, d=dst_ap, s=src_ap: g.dma_start(
                    out=d[:, n // 2:n], in_=s[:, n // 2:n]),
                 inc=sem, amt=16)
            return (sem, sem.n)

        WKV_K = SCR[:, 0:8192]
        WKV_V = SCR[:, 8192:16384]
        def WQ(slot): return SCR[:, 16384 + slot * 2048: 16384 + (slot + 1) * 2048]
        def WP(slot): return SCR[:, slot * 2048:(slot + 1) * 2048]
        def SQA(slot): return SCR[:, 8192 + (slot % 8) * 2048: 8192 + (slot % 8 + 1) * 2048]
        def WGB(slot): return SCR[:, slot * 2048:(slot + 1) * 2048]
        def WUB(slot): return SCR[:, 4096 + slot * 2048: 4096 + (slot + 1) * 2048]
        def WDB(slot): return SCR[:, 8192 + slot * 8192: 8192 + (slot + 1) * 8192]

        KTM = BUF2[:, 0:8192]
        VPAD = BUF2[:, 8192:16640]
        def ATTN(cc): return BUF2[:, cc * 2048:(cc + 1) * 2048]
        def YACC(half, ntl):
            return BUF2[:, half * 16384 + ntl * 2048: half * 16384 + (ntl + 1) * 2048]
        def HB(half, ft):
            base = (1 - half) * 16384
            return BUF2[:, base + ft * 1024: base + (ft + 1) * 1024]

        for cc in range(CT):
            dma(S.s_x, STREAM[:, cc * 2048:(cc + 1) * 2048],
                xT[cc * 128:(cc + 1) * 128, :])
        for d, s in [(IDN[:], id_d), (ERS[:], er_d), (FBS[:], fb_d),
                     (ONR[:], onr_d), (ONC[:], onc_d), (WRS[:], wr_d)]:
            dma(S.s_x, d, s)
        consts_done = (S.s_x, S.s_x.n)
        P.op("sync", lambda g: g.dma_start(out=BSB[:], in_=bp_d), inc=S.s_ld, amt=16)
        bp_done = S.s_ld.n
        P.op("vector", lambda v: v.memset(VPAD, 1.0), inc=S.s_vev, amt=1)
        vpad_ones = S.s_vev.n

        kv_group = 0
        wk_done = [None] * NG
        wv_done = [None] * NG
        wk_done[0] = dma2(S.s_wkvK, WKV_K, wk_d[0])
        wv_done[0] = dma2(S.s_wkvV, WKV_V, wv_d[0])

        for g in range(NG):
            for nt in range(NT):
                slot = kv_group % 2
                bank = ("PA", slot)
                ps = PA[:, slot * 512:(slot + 1) * 512]
                waits = [wk_done[g], consts_done] + P.bank_wait(bank)
                def mmk(t, ps=ps, nt=nt):
                    last = None
                    for cc in range(CT):
                        last = t.matmul(ps,
                                        STREAM[:, cc * 2048 + nt * 128: cc * 2048 + (nt + 1) * 128],
                                        WKV_K[:, cc * 512:(cc + 1) * 512],
                                        start=(cc == 0), stop=(cc == CT - 1))
                    return last
                P.op("tensor", mmk, waits=waits, inc=S.s_p1)
                need = S.s_p1.n
                P.op("scalar", lambda s, ps=ps, nt=nt: s.activation(
                        KTM[:, nt * 512:(nt + 1) * 512], ps, AF.Relu),
                     waits=[(S.s_p1, need), (S.s_vev, vpad_ones)], inc=S.s_kev)
                P.bank_free(bank, S.s_kev)
                kv_group += 1
            P.slot_free("wkvK", S.s_kev)
            if g + 1 < NG:
                wk_done[g + 1] = dma2(S.s_wkvK, WKV_K, wk_d[g + 1], extra_waits=P.slot_wait("wkvK"))
            for nt in range(NT):
                slot = kv_group % 2
                bank = ("PA", slot)
                ps = PA[:, slot * 512:(slot + 1) * 512]
                waits = [wv_done[g]] + P.bank_wait(bank)
                def mmv(t, ps=ps, nt=nt):
                    last = None
                    for cc in range(CT):
                        last = t.matmul(ps,
                                        STREAM[:, cc * 2048 + nt * 128: cc * 2048 + (nt + 1) * 128],
                                        WKV_V[:, cc * 512:(cc + 1) * 512],
                                        start=(cc == 0), stop=(cc == CT - 1))
                    return last
                P.op("tensor", mmv, waits=waits, inc=S.s_p1)
                need = S.s_p1.n
                def vev(v, ps=ps, nt=nt):
                    dst = VPAD[:, nt * 528:(nt + 1) * 528].rearrange(
                        "p (h e) -> p h e", e=33)[:, :, 0:32]
                    src = ps.rearrange("p (h e) -> p h e", e=32)
                    return v.tensor_copy(dst, src)
                P.op("vector", vev, waits=[(S.s_p1, need)], inc=S.s_vev)
                P.bank_free(bank, S.s_vev)
                kv_group += 1
            P.slot_free("wkvV", S.s_vev)
            if g + 1 < NG:
                wv_done[g + 1] = dma2(S.s_wkvV, WKV_V, wv_d[g + 1], extra_waits=P.slot_wait("wkvV"))
            kv_k_all, kv_v_all = S.s_kev.n, S.s_vev.n
            for hh in range(16):
                bsel = hh // 8
                bank = ("PB", bsel)
                col0 = bsel * 512 + (hh % 8) * 33
                ps = PB[0:32, col0: col0 + 33]
                waits = [(S.s_kev, kv_k_all), (S.s_vev, kv_v_all)]
                if hh % 8 == 0:
                    waits += P.bank_wait(bank)
                def mms(t, ps=ps, hh=hh):
                    last = None
                    for nt in range(NT):
                        last = t.matmul(ps,
                                        KTM[:, nt * 512 + 32 * hh: nt * 512 + 32 * hh + 32],
                                        VPAD[:, nt * 528 + 33 * hh: nt * 528 + 33 * hh + 33],
                                        start=(nt == 0), stop=(nt == NT - 1))
                    return last
                P.op("tensor", mms, waits=waits, inc=S.s_smm)
                if hh % 8 == 7:
                    need = S.s_smm.n
                    gh0 = 16 * g + (hh - 7)
                    def sev(v, bsel=bsel, gh0=gh0):
                        return v.tensor_copy(SFLAT[0:32, gh0 * 33:(gh0 + 8) * 33],
                                             PB[0:32, bsel * 512: bsel * 512 + 264])
                    P.op("vector", sev, waits=[(S.s_smm, need)], inc=S.s_sf)
                    P.bank_free(bank, S.s_sf)

        wq_done = [None] * JT
        wq_done[0] = dma(S.s_wq0, WQ(0), wq_d[0])
        wq_done[1] = dma(S.s_wq1, WQ(1), wq_d[1])

        NQ = JT * 4
        QPS_A2 = [PA[:, 0:512], PA[:, 512:1024], PDF[:, 0:512], PDF[:, 512:1024]]
        QBK_A2 = [("PA", 0), ("PA", 1), ("PD", 0), ("PD", 1)]
        qslot_of = lambda g: g % 4
        nslot_of = lambda g: g % 2
        def QS(g):
            k = g % 4
            return QSB[:, (k % 2) * 512:(k % 2 + 1) * 512] if k < 2 \
                else GS[:, (k - 2) * 512:(k - 1) * 512]

        def emit_snb(jt):
            sf_need = (4 * jt + 3) // 8 + 1
            snb_slot = jt % 2
            for r in range(4):
                gh = 4 * jt + r
                waits = [(S.s_sf, sf_need), consts_done]
                if r == 0:
                    waits += P.bank_wait(("PC", 0))
                def bnb(t, r=r, gh=gh):
                    return t.matmul(PC[:, 32 * r: 32 * r + 32],
                                    ERS[0:32, r * 128:(r + 1) * 128],
                                    SFLAT[0:32, gh * 33: gh * 33 + 32],
                                    start=True, stop=True)
                P.op("tensor", bnb, waits=waits, inc=S.s_nbm)
                def bkw(t, r=r, gh=gh):
                    return t.matmul(PC[:, 128 + r: 129 + r],
                                    ERS[0:32, r * 128:(r + 1) * 128],
                                    SFLAT[0:32, gh * 33 + 32: gh * 33 + 33],
                                    start=True, stop=True)
                P.op("tensor", bkw, inc=S.s_nbm)
            nb_need = S.s_nbm.n
            def snb_ev(v, snb_slot=snb_slot):
                return v.tensor_copy(SNB[:, snb_slot * 132:(snb_slot + 1) * 132],
                                     PC[:, 0:132])
            P.op("vector", snb_ev,
                 waits=[(S.s_nbm, nb_need)] + P.slot_wait(("SNB", snb_slot)),
                 inc=S.s_nbe)
            P.bank_free(("PC", 0), S.s_nbe)
            return S.s_nbe.n

        nbe_need = [0] * JT
        qmm_need = [0] * NQ
        qrl_need = [0] * NQ
        nmm_need = [0] * NQ
        dmm_need = [0] * NQ
        dev_need = [0] * NQ
        bcm_need = [0] * NQ
        bce_need = [0] * NQ

        def emit_q(g):
            jt, nc4 = g // 4, g % 4
            qs = qslot_of(g)
            ps_q = QPS_A2[qs]
            waits = [wq_done[jt]] + P.bank_wait(QBK_A2[qs])
            def mmq(t, ps=ps_q, jt=jt, nc4=nc4):
                last = None
                for cc in range(CT):
                    last = t.matmul(ps,
                                    WQ(jt % 2)[:, cc * 128:(cc + 1) * 128],
                                    STREAM[:, cc * 2048 + nc4 * 512: cc * 2048 + (nc4 + 1) * 512],
                                    start=(cc == 0), stop=(cc == CT - 1))
                return last
            P.op("tensor", mmq, waits=waits, inc=S.s_qmm)
            qmm_need[g] = S.s_qmm.n
            P.op("scalar", lambda s, ps=ps_q, g=g: s.activation(
                    QS(g), ps, AF.Relu),
                 waits=[(S.s_qmm, qmm_need[g])] + P.slot_wait(("QSB", g % 4)),
                 inc=S.s_qrl)
            P.bank_free(QBK_A2[qs], S.s_qrl)
            qrl_need[g] = S.s_qrl.n

        def emit_numden(g):
            jt = g // 4
            qs = qslot_of(g)
            ns = nslot_of(g)
            snb_slot = jt % 2
            def mmn(t, g=g, ns=ns, snb_slot=snb_slot):
                return t.matmul(PB[:, ns * 512:(ns + 1) * 512],
                                SNB[:, snb_slot * 132: snb_slot * 132 + 128],
                                QS(g),
                                start=True, stop=True)
            P.op("tensor", mmn,
                 waits=[(S.s_qrl, qrl_need[g]), (S.s_nbe, nbe_need[jt])]
                       + P.bank_wait(("PB", ns)), inc=S.s_nmm)
            nmm_need[g] = S.s_nmm.n
            def mmd(t, g=g, snb_slot=snb_slot):
                return t.matmul(PC[0:4, 0:512],
                                SNB[:, snb_slot * 132 + 128: snb_slot * 132 + 132],
                                QS(g),
                                start=True, stop=True)
            P.op("tensor", mmd, waits=P.bank_wait(("PC", 0)), inc=S.s_dmm)
            dmm_need[g] = S.s_dmm.n
            P.slot_free(("QSB", g % 4), S.s_dmm)
            P.op("vector", lambda v: v.tensor_single_scalar(
                    DENS[:], PC[0:4, 0:512], EPS_ATTN, ALU.add),
                 waits=[(S.s_dmm, dmm_need[g])] + P.slot_wait("DENS"))
            def drec(v):
                with nc.allow_low_precision(reason="bf16 attn denominator recip"):
                    return v.reciprocal(DENS[:], DENS[:])
            P.op("vector", drec, inc=S.s_dev)
            P.bank_free(("PC", 0), S.s_dev)
            dev_need[g] = S.s_dev.n

        def emit_bc(g):
            jt, nc4 = g // 4, g % 4
            ns = nslot_of(g)
            def mmb(t):
                return t.matmul(PC[:, 512:1024], FBS[:], DENS[:],
                                start=True, stop=True)
            P.op("tensor", mmb,
                 waits=[(S.s_dev, dev_need[g])] + P.bank_wait(("PC", 1)), inc=S.s_bcm)
            bcm_need[g] = S.s_bcm.n
            P.slot_free("DENS", S.s_bcm)
            P.op("vector", lambda v: v.tensor_copy(DENB[:], PC[:, 512:1024]),
                 waits=[(S.s_bcm, bcm_need[g])] + P.slot_wait("DENB"), inc=S.s_bce)
            P.bank_free(("PC", 1), S.s_bce)
            bce_need[g] = S.s_bce.n
            def att(v, jt=jt, nc4=nc4, ns=ns):
                return v.tensor_tensor(
                    ATTN(jt)[:, nc4 * 512:(nc4 + 1) * 512],
                    DENB[:], PB[:, ns * 512:(ns + 1) * 512], ALU.mult)
            P.op("vector", att, waits=[(S.s_bce, bce_need[g]), (S.s_nmm, nmm_need[g])],
                 inc=S.s_att)
            P.bank_free(("PB", ns), S.s_att)
            P.slot_free("DENB", S.s_att)

        for g in range(NQ):
            jt = g // 4
            if g == 0:
                nbe_need[0] = emit_snb(0)
            if g % 4 == 3 and jt + 1 < JT:
                nbe_need[jt + 1] = emit_snb(jt + 1)
            if g >= 4:
                emit_bc(g - 4)
            if g >= 3:
                emit_numden(g - 3)
            emit_q(g)
            if g % 4 == 3:
                P.slot_free(("WQ", jt % 2), S.s_qrl)
                if jt + 2 < JT:
                    wq_done[jt + 2] = dma(S.s_wq0 if jt % 2 == 0 else S.s_wq1,
                                          WQ(jt % 2), wq_d[jt + 2],
                                          extra_waits=P.slot_wait(("WQ", jt % 2)))
        for g in range(NQ - 3, NQ):
            emit_bc(g - 1)
            emit_numden(g)
        emit_bc(NQ - 1)
        attn_all = S.s_att.n

        if debug:
            P.op("gpsimd", lambda g: g.dma_start(out=dbg["attn"], in_=BUF2[:, 0:CT * 2048]),
                 waits=[(S.s_att, attn_all)], inc=S.s_dbg, amt=16)

        wp_done = [None] * CT
        wp_done[0] = dma(S.s_wp0, WP(0), wp_d[0],
                         extra_waits=[(S.s_kev, S.s_kev.n), (S.s_vev, S.s_vev.n)])
        wp_done[1] = dma(S.s_wp1, WP(1), wp_d[1],
                         extra_waits=[(S.s_kev, S.s_kev.n), (S.s_vev, S.s_vev.n)])

        _tr_idx = [0]
        def emit_yinit_nts(half, nts, ap_val):
            for nt in nts:
                ntl = nt - half * NTH
                for ccg in range(4):
                    slot = _tr_idx[0] % 2
                    bank = ("PD", slot)
                    waits = [(S.s_ap, ap_val), consts_done] + P.bank_wait(bank)
                    def trp4(t, slot=slot, ccg=ccg, nt=nt):
                        last = None
                        for k in range(4):
                            cc = ccg * 4 + k
                            last = t.transpose(
                                PD[:, slot * 1024 + k * 128: slot * 1024 + (k + 1) * 128],
                                STREAM[:, cc * 2048 + nt * 128: cc * 2048 + (nt + 1) * 128],
                                IDN[:])
                        return last
                    P.op("tensor", trp4, waits=waits, inc=S.s_tr)
                    tr_need = S.s_tr.n
                    def trev(v, slot=slot, half=half, ntl=ntl, ccg=ccg):
                        return v.tensor_copy(
                            YACC(half, ntl)[:, ccg * 512:(ccg + 1) * 512],
                            PD[:, slot * 1024: slot * 1024 + 512])
                    P.op("vector", trev, waits=[(S.s_tr, tr_need)], inc=S.s_tre)
                    P.bank_free(bank, S.s_tre)
                    _tr_idx[0] += 1

        PROJ_PS = [PA[0:1, 0:512], PA[0:1, 512:1024], PB[0:1, 0:512], PB[0:1, 512:1024]]
        PROJ_PSF = [PA[:, 0:512], PA[:, 512:1024], PB[:, 0:512], PB[:, 512:1024]]
        PROJ_BANK = [("PA", 0), ("PA", 1), ("PB", 0), ("PB", 1)]
        for cot in range(CT):
            for q in range(4):
                ps = PROJ_PSF[q]
                waits = [wp_done[cot], (S.s_att, attn_all)] + P.bank_wait(PROJ_BANK[q])
                def mmp(t, ps=ps, cot=cot, q=q):
                    last = None
                    for cc in range(CT):
                        last = t.matmul(ps,
                                        WP(cot % 2)[:, cc * 128:(cc + 1) * 128],
                                        ATTN(cc)[:, q * 512:(q + 1) * 512],
                                        start=(cc == 0), stop=(cc == CT - 1))
                    return last
                P.op("tensor", mmp, waits=waits, inc=S.s_pmm)
                pmm_need = S.s_pmm.n
                def pev(v, ps=ps, cot=cot, q=q):
                    dst = STREAM[:, cot * 2048 + q * 512: cot * 2048 + (q + 1) * 512]
                    return v.scalar_tensor_tensor(dst, ps, BSB[:, cot:cot + 1], dst,
                                                  ALU.add, ALU.add)
                P.op("vector", pev, waits=[(S.s_pmm, pmm_need), (S.s_ld, bp_done)],
                     inc=S.s_pev)
                P.bank_free(PROJ_BANK[q], S.s_pev)
            P.slot_free(("WP", cot % 2), S.s_pev)
            if cot + 2 < CT:
                wp_done[cot + 2] = dma(S.s_wp0 if cot % 2 == 0 else S.s_wp1,
                                       WP(cot % 2), wp_d[cot + 2],
                                       extra_waits=P.slot_wait(("WP", cot % 2)))
        pev_all = S.s_pev.n

        SQPS = [PC[0:1, 0:512], PC[0:1, 512:1024], PDF[0:1, 0:512], PDF[0:1, 512:1024]]
        SQBANK = [("PC", 0), ("PC", 1), ("PD", 0), ("PD", 1)]
        sq_done = [0] * CT
        for cc in range(CT):
            def sqr(v, cc=cc):
                sl = STREAM[:, cc * 2048:(cc + 1) * 2048]
                return v.tensor_tensor(SQA(cc), sl, sl, ALU.mult)
            waits = [(S.s_pev, 4 * (cc + 1))]
            if cc >= 8:
                waits.append((S.s_sqc, cc - 7))
            P.op("vector", sqr, waits=waits, inc=S.s_sq)
            sq_done[cc] = S.s_sq.n
        for cc in range(CT):
            waits = [(S.s_pev, pev_all)]
            if cc == 0:
                for q in range(4):
                    waits += P.bank_wait(PROJ_BANK[q])
            def mmsx(t, cc=cc):
                last = None
                for q in range(4):
                    last = t.matmul(PROJ_PS[q], ONC[:],
                                    STREAM[:, cc * 2048 + q * 512: cc * 2048 + (q + 1) * 512],
                                    start=(cc == 0), stop=(cc == CT - 1))
                return last
            P.op("tensor", mmsx, waits=waits,
                 inc=(S.s_sxm if cc == CT - 1 else None))
            waits = [(S.s_sq, sq_done[cc])]
            if cc == 0:
                for q in range(4):
                    waits += P.bank_wait(SQBANK[q])
            def mmsq(t, cc=cc):
                last = None
                for q in range(4):
                    last = t.matmul(SQPS[q], ONC[:],
                                    SQA(cc)[:, q * 512:(q + 1) * 512],
                                    start=(cc == 0), stop=(cc == CT - 1))
                return last
            inc = S.s_sqc if cc < CT - 1 else S.s_sqm
            P.op("tensor", mmsq, waits=waits, inc=inc)
        sx_need = S.s_sxm.n
        sqm_need = S.s_sqm.n
        for q in range(4):
            P.op("vector", lambda v, q=q: v.tensor_single_scalar(
                    MROW[0:1, q * 512:(q + 1) * 512], PROJ_PS[q], 1.0 / C, ALU.mult),
                 waits=[(S.s_sxm, sx_need)], inc=S.s_row)
            P.bank_free(PROJ_BANK[q], S.s_row)
        MSQV = OUT_SB[0:1, :]
        row_need0 = S.s_row.n
        P.op("vector", lambda v: v.tensor_tensor(MSQV, MROW[0:1, :], MROW[0:1, :], ALU.mult),
             waits=[(S.s_row, row_need0), (S.s_sqm, sqm_need)])
        for q in range(4):
            def vvar(v, q=q):
                return v.scalar_tensor_tensor(MSQV[0:1, q * 512:(q + 1) * 512],
                                              SQPS[q], 1.0 / C,
                                              MSQV[0:1, q * 512:(q + 1) * 512],
                                              ALU.mult, ALU.subtract)
            P.op("vector", vvar, inc=S.s_row)
            P.bank_free(SQBANK[q], S.s_row)
        P.op("vector", lambda v: v.tensor_single_scalar(MSQV, MSQV, EPS_LN, ALU.add),
             inc=S.s_row)
        row_need = S.s_row.n
        P.op("scalar", lambda s: s.activation(RROW[0:1, :], MSQV, AF.Sqrt),
             waits=[(S.s_row, row_need)], inc=S.s_sq)
        sd_need = S.s_sq.n
        def rrec(v):
            with nc.allow_low_precision(reason="bf16 rstd recip"):
                return v.reciprocal(RROW[0:1, :], RROW[0:1, :])
        P.op("vector", rrec, waits=[(S.s_sq, sd_need)], inc=S.s_row)
        rowm_need = S.s_row.n

        BCPAIRS = [(("PA", 0), ("PA", 1)), (("PB", 0), ("PB", 1)),
                   (("PC", 0), ("PC", 1)), (("PC", 0), ("PC", 1))]
        BCPS = [(PA[:, 0:512], PA[:, 512:1024]), (PB[:, 0:512], PB[:, 512:1024]),
                (PC[:, 0:512], PC[:, 512:1024]), (PC[:, 0:512], PC[:, 512:1024])]
        MBC = DENB[:]
        RBC = QSB[:, 0:512]
        for q in range(4):
            bm, br = BCPAIRS[q]
            pm, pr = BCPS[q]
            def mmbm(t, q=q, pm=pm):
                return t.matmul(pm, ONR[:], MROW[0:1, q * 512:(q + 1) * 512],
                                start=True, stop=True)
            P.op("tensor", mmbm, waits=[(S.s_row, rowm_need)] + P.bank_wait(bm),
                 inc=S.s_bc2)
            bcm_v = S.s_bc2.n
            def mmbr(t, q=q, pr=pr):
                return t.matmul(pr, ONR[:], RROW[0:1, q * 512:(q + 1) * 512],
                                start=True, stop=True)
            P.op("tensor", mmbr, waits=P.bank_wait(br), inc=S.s_bc2)
            bcr_v = S.s_bc2.n
            P.op("scalar", lambda s, pm=pm: s.copy(MBC, pm),
                 waits=[(S.s_bc2, bcm_v)] + P.slot_wait("MBC"), inc=S.s_sq)
            P.bank_free(bm, S.s_sq)
            P.op("scalar", lambda s, pr=pr: s.copy(RBC, pr),
                 waits=[(S.s_bc2, bcr_v)] + P.slot_wait("RBC"), inc=S.s_sq)
            P.bank_free(br, S.s_sq)
            if q in (1, 2):
                emit_yinit_nts(0, range(4 * (q - 1), 4 * q), 16 * q)
            bce_v = S.s_sq.n
            for cc in range(CT):
                def app1(v, cc=cc, q=q):
                    return v.tensor_tensor(TAPP,
                                           STREAM[:, cc * 2048 + q * 512: cc * 2048 + (q + 1) * 512],
                                           MBC, ALU.subtract)
                P.op("vector", app1, waits=[(S.s_sq, bce_v)])
                def app2(v, cc=cc, q=q):
                    return v.tensor_tensor(
                        STREAM[:, cc * 2048 + q * 512: cc * 2048 + (q + 1) * 512],
                        TAPP, RBC, ALU.mult)
                P.op("vector", app2, inc=S.s_ap)
            P.slot_free("MBC", S.s_ap)
            P.slot_free("RBC", S.s_ap)
        ap_all = S.s_ap.n

        if debug:
            P.op("gpsimd", lambda g: g.dma_start(out=dbg["x1"], in_=STREAM[:, 0:CT * 2048]),
                 waits=[(S.s_ap, ap_all)], inc=S.s_dbg, amt=16)

        rmm_vals = [0, 0]
        router_state = {}
        def emit_router_scores():
            for half_b in range(2):
                bank = ("PC", half_b)
                waits0 = [(S.s_ap, ap_all), consts_done] + P.bank_wait(bank)
                def mmr_all(t, half_b=half_b, waits_done=None):
                    last = None
                    for ntl in range(8):
                        nt = half_b * 8 + ntl
                        for cc in range(CT):
                            last = t.matmul(
                                PC[0:128, half_b * 512 + ntl * 8: half_b * 512 + ntl * 8 + 8],
                                STREAM[:, cc * 2048 + nt * 128: cc * 2048 + (nt + 1) * 128],
                                WRS[:, cc * 8:(cc + 1) * 8],
                                start=(cc == 0), stop=(cc == CT - 1))
                    return last
                P.op("tensor", mmr_all, waits=waits0, inc=S.s_rmm)
                rmm_vals[half_b] = S.s_rmm.n
            for half_b in range(2):
                P.op("vector", lambda v, half_b=half_b: v.tensor_copy(
                        SCSB[:, half_b * 64:(half_b + 1) * 64],
                        PC[0:128, half_b * 512: half_b * 512 + 64]),
                     waits=[(S.s_rmm, rmm_vals[half_b])], inc=S.s_re)
                P.bank_free(("PC", half_b), S.s_re)
            sc_done = S.s_re.n
            router_state["sc_done"] = sc_done

        def emit_router_gates(nts):
            sc_done = router_state["sc_done"]
            for nt in nts:
                sc = SCSB[:, nt * 8:(nt + 1) * 8]
                def vch(fn, extra=()):
                    P.op("vector", fn, waits=list(extra) + [(S.s_vch, S.s_vch.n)], inc=S.s_vch)
                vch(lambda v, sc=sc: v.reduce_max(SM1[:], sc, axis=AX.X, negate=True),
                    [(S.s_re, sc_done)])
                re_need = S.s_vch.n
                P.op("scalar", lambda s, sc=sc: s.activation(SE8[:], sc, AF.Exp, bias=SM1[:]),
                     waits=[(S.s_vch, re_need)], inc=S.s_sq)
                e_need = S.s_sq.n
                vch(lambda v: v.reduce_max(SM2[:], SE8[:], axis=AX.X), [(S.s_sq, e_need)])
                vch(lambda v: v.tensor_scalar(ST8[:], SE8[:], SM2[:], None, ALU.is_ge))
                vch(lambda v: v.scalar_tensor_tensor(ST8[:], ST8[:], -2.0, SE8[:],
                                                     ALU.mult, ALU.add))
                vch(lambda v: v.reduce_max(SM3[:], ST8[:], axis=AX.X))
                vch(lambda v: v.tensor_scalar(SL8[:], SE8[:], SM3[:], None, ALU.is_ge))
                vch(lambda v: v.tensor_tensor(SM2[:], SM2[:], SM3[:], ALU.add))
                vch(lambda v: v.reciprocal(SM2[:], SM2[:]))
                vch(lambda v: v.tensor_tensor(ST8[:], SE8[:], SL8[:], ALU.mult))
                vch(lambda v, nt=nt: v.tensor_scalar(
                        WSB[:, nt * 8:(nt + 1) * 8], ST8[:], SM2[:], None, ALU.mult))

            router_state["done"] = S.s_vch.n
            if debug:
                P.op("gpsimd", lambda g: g.dma_start(out=dbg["w"], in_=WSB[:]),
                     waits=[(S.s_vch, router_state["done"])], inc=S.s_dbg, amt=16)

        wg_done = {}
        wu_done = {}
        wd_done = {}

        def emit_wgu(eh, ft):
            half, ei = eh // NE, eh % NE
            gi = eh * NF + ft
            waits = []
            if gi >= 2:
                waits.append((S.s_sil, 2 * gi - 2))
            else:
                waits.append((S.s_pev, pev_all))
            wg_done[eh, ft] = dma(S.s_wg0 if ft % 2 == 0 else S.s_wg1,
                                  WGB(ft % 2), wg_d[ei, ft], extra_waits=waits)
            waits = []
            if gi >= 2:
                waits.append((S.s_h, 2 * gi - 2))
            else:
                waits.append((S.s_kev, 64))
            wu_done[eh, ft] = dma(S.s_wu0 if ft % 2 == 0 else S.s_wu1,
                                  WUB(ft % 2), wu_d[ei, ft], extra_waits=waits)

        def emit_wd(eh, cq):
            half, ei = eh // NE, eh % NE
            di = eh * NCQ + cq
            waits = []
            if di >= 2:
                waits.append((S.s_yev, 8 * di - 8))
            else:
                waits.append((S.s_qrl, 4 * JT))
                waits.append((S.s_sqm, sqm_need))
                waits.append((S.s_vev, 65))
            wd_done[eh, cq] = dma2(S.s_wd0 if cq % 2 == 0 else S.s_wd1,
                                   WDB(cq % 2), wd_d[ei, cq], extra_waits=waits)

        for eh in range(2 * NE):
            if eh == 0:
                emit_wgu(0, 0)
                emit_wgu(0, 1)
            for ft in range(2, NF):
                emit_wgu(eh, ft)
            emit_wd(eh, 0)
            emit_wd(eh, 1)
            if eh + 1 < 2 * NE:
                emit_wgu(eh + 1, 0)
                emit_wgu(eh + 1, 1)
            emit_wd(eh, 2)
            emit_wd(eh, 3)

        ydone = np.zeros((2, NTH), dtype=np.int64)
        cact_base = [0, 0]


        def emit_half_experts(half, c_half=None, router=False, yinit1=False):
            for ei in range(NE):
                eh = half * NE + ei
                for ft in range(NF):
                    if yinit1 and ei == 0 and ft < NTH:
                        emit_yinit_nts(1, [NTH + ft], ap_all)
                    if c_half is not None and ei == 0 and ft % 2 == 0:
                        emit_c_tile(c_half, ft // 2)
                    if router and ei == 0 and ft == 2:
                        emit_router_scores()
                        emit_router_gates([0, 1, 2])
                    elif router and ei == 0 and 2 < ft < NF:
                        emit_router_gates([ft])
                    waits = ([wg_done[eh, ft]]
                             + P.bank_wait(("PA", 0)) + P.bank_wait(("PA", 1)))
                    if eh == 0:
                        waits.append((S.s_ap, 32))
                    umerge = {}
                    for (s, v) in ([wu_done[eh, ft]] + P.bank_wait(("PB", 0))
                                   + P.bank_wait(("PB", 1))):
                        if s is not None and v > 0:
                            if umerge.get(id(s), (None, -1))[1] < v:
                                umerge[id(s)] = (s, v)
                    u_waits = list(umerge.values())
                    S.s_gmm.n += 2
                    gneed = (S.s_gmm.n - 1, S.s_gmm.n)
                    S.s_umm.n += 2
                    uneed = (S.s_umm.n - 1, S.s_umm.n)
                    def mmgu(t, ft=ft, half=half, gsem=S.s_gmm, usem=S.s_umm,
                             u_waits=u_waits):
                        lasts = [None, None]
                        for dc in range(CT):
                            for q in range(2):
                                lasts[q] = t.matmul(
                                    PA[:, q * 512:(q + 1) * 512],
                                    WGB(ft % 2)[:, dc * 128:(dc + 1) * 128],
                                    STREAM[:, dc * 2048 + half * 1024 + q * 512:
                                           dc * 2048 + half * 1024 + (q + 1) * 512],
                                    start=(dc == 0), stop=(dc == CT - 1))
                        lasts[0].then_inc(gsem.h, 1)
                        lasts[1].then_inc(gsem.h, 1)
                        for (s, v) in u_waits:
                            t.wait_ge(s.h, v)
                        for dc in range(CT):
                            for q in range(2):
                                lasts[q] = t.matmul(
                                    PB[:, q * 512:(q + 1) * 512],
                                    WUB(ft % 2)[:, dc * 128:(dc + 1) * 128],
                                    STREAM[:, dc * 2048 + half * 1024 + q * 512:
                                           dc * 2048 + half * 1024 + (q + 1) * 512],
                                    start=(dc == 0), stop=(dc == CT - 1))
                        lasts[0].then_inc(usem.h, 1)
                        lasts[1].then_inc(usem.h, 1)
                        return None
                    P.op("tensor", mmgu, waits=waits)
                    sil_needs = []
                    for q in range(2):
                        def sil(s, q=q):
                            return s.activation(GS[:, q * 512:(q + 1) * 512],
                                                PA[:, q * 512:(q + 1) * 512], AF.Silu)
                        P.op("scalar", sil,
                             waits=[(S.s_gmm, gneed[q])] + P.slot_wait(("GS", q)),
                             inc=S.s_sil)
                        P.bank_free(("PA", q), S.s_sil)
                        sil_needs.append(S.s_sil.n)
                    for q in range(2):
                        hwaits = [(S.s_sil, sil_needs[q]), (S.s_umm, uneed[q])]
                        if half == 1:
                            hwaits.append((S.s_cact, cact_base[0] + ft // 2 + 1))
                        def hev(v, half=half, ft=ft, q=q):
                            return v.tensor_tensor(
                                HB(half, ft)[:, q * 512:(q + 1) * 512],
                                GS[:, q * 512:(q + 1) * 512],
                                PB[:, q * 512:(q + 1) * 512], ALU.mult)
                        P.op("vector", hev, waits=hwaits, inc=S.s_h)
                        P.bank_free(("PB", q), S.s_h)
                        P.slot_free(("GS", q), S.s_h)
                h_all = S.s_h.n
                YPS = [PC[:, 0:512], PC[:, 512:1024], PDF[0:128, 0:512], PDF[0:128, 512:1024]]
                YBK = [("PC", 0), ("PC", 1), ("PD", 0), ("PD", 1)]
                for cq in range(NCQ):
                    for ntl0 in range(0, NTH, 2):
                        k = S.s_ymm.n
                        waits = [wd_done[eh, cq], (S.s_h, h_all - 2)]
                        waits += P.bank_wait(YBK[k % 4])
                        j1_waits = [(s, v) for (s, v) in P.bank_wait(YBK[(k + 1) % 4])
                                    if s is not None and v > 0]
                        S.s_ymm.n += 2
                        yneed = (S.s_ymm.n - 1, S.s_ymm.n)
                        def mmy2(t, half=half, ntl0=ntl0, cq=cq, k=k, sem=S.s_ymm,
                                 j1_waits=j1_waits, hsem=S.s_h, h_all=h_all):
                            for j in range(2):
                                if j == 1:
                                    for (s, v) in j1_waits:
                                        t.wait_ge(s.h, v)
                                ntl = ntl0 + j
                                ps = YPS[(k + j) % 4]
                                last = None
                                for fc in range(NF):
                                    if j == 0 and fc == NF - 1:
                                        t.wait_ge(hsem.h, h_all)
                                    last = t.matmul(ps,
                                                    HB(half, fc)[:, ntl * 128:(ntl + 1) * 128],
                                                    WDB(cq % 2)[:, fc * 512:(fc + 1) * 512],
                                                    start=(fc == 0), stop=(fc == NF - 1))
                                last.then_inc(sem.h, 1)
                            return None
                        P.op("tensor", mmy2, waits=waits)
                        for j in range(2):
                            ntl = ntl0 + j
                            ps_y = YPS[(k + j) % 4]
                            nt = half * NTH + ntl
                            ywaits = [(S.s_ymm, yneed[j]), (S.s_vch, router_state["done"]),
                                      (S.s_tre, (half + 1) * NTH * 4)]
                            def yev(v, ps=ps_y, half=half, ntl=ntl, cq=cq, nt=nt, ei=ei):
                                dst = YACC(half, ntl)[:, cq * 512:(cq + 1) * 512]
                                return v.scalar_tensor_tensor(
                                    dst, ps, WSB[:, nt * 8 + ei: nt * 8 + ei + 1], dst,
                                    ALU.mult, ALU.add)
                            P.op("vector", yev, waits=ywaits, inc=S.s_yev)
                            P.bank_free(YBK[(k + j) % 4], S.s_yev)
                            if ei == NE - 1 and cq == NCQ - 1:
                                ydone[half, ntl] = S.s_yev.n

        OUT2 = SCR[:, 0:4096].bitcast(F32)
        def emit_c_tile(half, ntl):
                nt = half * NTH + ntl
                k = nt
                ytile = YACC(half, ntl)
                OUTB = OUT_SB[:] if (half == 0 or k % 2 == 0) else OUT2
                def cch(fn, extra=()):
                    P.op("vector", fn, waits=list(extra) + [(S.s_vch, S.s_vch.n)],
                         inc=S.s_vch)
                w0 = [(S.s_yev, int(ydone[half, ntl]))]
                if half == 0 and k > 0:
                    w0.append((S.s_cact, k))
                cch(lambda v, ytile=ytile: v.reduce_sum(CS[:], ytile, axis=AX.X), w0)
                cch(lambda v: v.tensor_single_scalar(CM[:], CS[:], 1.0 / C, ALU.mult))
                sqw = [(S.s_yev, int(ydone[half, ntl]))]
                if half == 0:
                    if k >= 1:
                        sqw.append((S.s_out, 16 * k))
                else:
                    prev = k - 2 if k >= 10 else (7 if k % 2 == 0 else -1)
                    if prev >= 0:
                        sqw.append((S.s_out, 16 * (prev + 1)))
                P.op("scalar", lambda s, ytile=ytile, OUTB=OUTB: s.activation(
                        OUTB, ytile, AF.Square), waits=sqw, inc=S.s_csq)
                csq_need = S.s_csq.n
                cch(lambda v, OUTB=OUTB: v.reduce_sum(CS[:], OUTB, axis=AX.X),
                    [(S.s_csq, csq_need)])
                cch(lambda v: v.tensor_tensor(CV[:], CM[:], CM[:], ALU.mult))
                cch(lambda v: v.scalar_tensor_tensor(CV[:], CS[:], 1.0 / C, CV[:],
                                                     ALU.mult, ALU.subtract))
                cch(lambda v: v.tensor_single_scalar(CV[:], CV[:], EPS_LN, ALU.add))
                cr2b = S.s_vch.n
                P.op("scalar", lambda s: s.activation(CS[:], CV[:], AF.Sqrt),
                     waits=[(S.s_vch, cr2b)], inc=S.s_csq)
                csq2 = S.s_csq.n
                cch(lambda v: v.reciprocal(CS[:], CS[:]), [(S.s_csq, csq2)])
                P.op("vector", lambda v, ytile=ytile, OUTB=OUTB: v.tensor_scalar(
                        OUTB, ytile, CM[:], CS[:], ALU.subtract, ALU.mult),
                     waits=[(S.s_vch, S.s_vch.n)], inc=S.s_cact)
                cact_need = S.s_cact.n
                def outdma(g, nt=nt, OUTB=OUTB):
                    return g.dma_start(out=out_d[nt * 128:(nt + 1) * 128, :],
                                       in_=OUTB)
                P.op("sync", outdma, waits=[(S.s_cact, cact_need)], inc=S.s_out, amt=16)

        emit_half_experts(0, router=True)
        cact_base[0] = S.s_cact.n
        emit_half_experts(1, c_half=0, yinit1=True)
        for ntl in range(NTH):
            emit_c_tile(1, ntl)

        final_out = S.s_out.n
        final_dbg = S.s_dbg.n
        def fin(g):
            g.wait_ge(S.s_out.h, final_out)
            if debug and final_dbg:
                g.wait_ge(S.s_dbg.h, final_dbg)
            return None
        P.op("sync", fin)

        with nc.Block() as block:
            @block.tensor
            def _(eng):
                P.run("tensor", eng)

            @block.vector
            def _(eng):
                P.run("vector", eng)

            @block.scalar
            def _(eng):
                P.run("scalar", eng)

            @block.gpsimd
            def _(eng):
                P.run("gpsimd", eng)

            @block.sync
            def _(eng):
                P.run("sync", eng)

    return nc



def _pack(np_inputs):
    f32 = np.float32
    wqkv = np.asarray(np_inputs["L0_attn_w_qkv"], f32)
    wproj = np.asarray(np_inputs["L0_attn_w_proj"], f32)
    bproj = np.asarray(np_inputs["L0_attn_b_proj"], f32)
    Wr = np.asarray(np_inputs["L0_ffn_Wr"], f32)
    Wg = np.asarray(np_inputs["L0_ffn_Wg"], f32)
    Wu = np.asarray(np_inputs["L0_ffn_Wu"], f32)
    Wd = np.asarray(np_inputs["L0_ffn_Wd"], f32)

    d = {}
    d["wq"] = np.ascontiguousarray(
        wqkv[0:2048].reshape(16, 128, 16, 128).transpose(0, 3, 2, 1)).reshape(16, 128, 2048)
    d["wk"] = np.ascontiguousarray(
        wqkv[2048:4096].reshape(4, 512, 16, 128).transpose(0, 3, 2, 1)).reshape(4, 128, 8192)
    d["wv"] = np.ascontiguousarray(
        wqkv[4096:6144].reshape(4, 512, 16, 128).transpose(0, 3, 2, 1)).reshape(4, 128, 8192)
    d["wp"] = np.ascontiguousarray(
        wproj.reshape(16, 128, 16, 128).transpose(0, 3, 2, 1)).reshape(16, 128, 2048)
    d["bp"] = np.ascontiguousarray(bproj.reshape(16, 128).T)
    d["wr"] = np.ascontiguousarray(Wr.reshape(16, 128, 8).transpose(1, 0, 2)).reshape(128, 128)
    d["wg"] = np.ascontiguousarray(
        Wg.reshape(8, 16, 128, 16, 128).transpose(0, 3, 2, 1, 4)).reshape(8, 16, 128, 2048)
    d["wu"] = np.ascontiguousarray(
        Wu.reshape(8, 16, 128, 16, 128).transpose(0, 3, 2, 1, 4)).reshape(8, 16, 128, 2048)
    d["wd"] = np.ascontiguousarray(
        Wd.reshape(8, 16, 128, 4, 512).transpose(0, 3, 2, 1, 4)).reshape(8, 4, 128, 8192)
    d["id_in"] = np.eye(128, dtype=f32)
    er = np.zeros((32, 512), f32)
    for r in range(4):
        for dd in range(32):
            er[dd, r * 128 + 32 * r + dd] = 1.0
    d["er_in"] = er
    fb = np.zeros((4, 128), f32)
    for r in range(4):
        fb[r, 32 * r:32 * r + 32] = 1.0
    d["fb_in"] = fb
    d["onr_in"] = np.ones((1, 128), f32)
    d["onc_in"] = np.ones((128, 1), f32)
    return d


def kernel(L0_attn_x, L0_attn_w_qkv, L0_attn_w_proj, L0_attn_b_proj,
           L0_ffn_Wr, L0_ffn_Wg, L0_ffn_Wu, L0_ffn_Wd):
    x = np.asarray(L0_attn_x, np.float32)
    shared = _pack({
        "L0_attn_w_qkv": L0_attn_w_qkv, "L0_attn_w_proj": L0_attn_w_proj,
        "L0_attn_b_proj": L0_attn_b_proj, "L0_ffn_Wr": L0_ffn_Wr,
        "L0_ffn_Wg": L0_ffn_Wg, "L0_ffn_Wu": L0_ffn_Wu, "L0_ffn_Wd": L0_ffn_Wd,
    })
    nc = build_program(debug=False)
    in_maps = []
    for b in range(8):
        m = dict(shared)
        m["xT"] = np.ascontiguousarray(x[b].T)
        in_maps.append(m)
    res = run_bass_kernel_spmd(nc, in_maps, core_ids=list(range(8)))
    out = np.stack([np.asarray(res.results[b]["out"], np.float32) for b in range(8)], axis=0)
    return out
